# version 04
# baseline (speedup 1.0000x reference)
import numpy as np
import contextlib
import concourse.bass as bass
import concourse.mybir as mybir
from concourse.bass_utils import run_bass_kernel_spmd

F32 = mybir.dt.float32
BF16 = mybir.dt.bfloat16
I32 = mybir.dt.int32
AF = mybir.ActivationFunctionType
ALU = mybir.AluOpType

P = 128
D = 1024
NCH = 8
NH = 16
HD = 64
VW = NH * (HD + 1)
C_L0A = 4.0
C_L0F = 12.0
C_L1A = 12.0
EPS = 1e-5



def _tw(w):
    din, dout = w.shape
    return np.ascontiguousarray(w.reshape(din // P, P, dout // P, P).transpose(0, 2, 1, 3)).reshape(-1, P, P)


def _trhs(w, n=512):
    din, dout = w.shape
    n = min(n, dout)
    return np.ascontiguousarray(
        w.reshape(din // P, P, dout // n, n).transpose(0, 2, 1, 3)).reshape(-1, P, n)


def _twm(w):
    din, dout = w.shape
    return np.ascontiguousarray(
        w.reshape(din // P, P, dout // P, P).transpose(2, 0, 1, 3)).reshape(-1, P, P)


def _bpm(b):
    return np.ascontiguousarray(b.reshape(-1, P).T)


def _who(w):
    return np.ascontiguousarray(
        w.reshape(NH, HD, NCH, P).transpose(2, 0, 1, 3)).reshape(-1, HD, P)


def own_rows(parity, T):
    B = min(512, T)
    return np.concatenate([np.arange((2 * i + parity) * B, (2 * i + parity + 1) * B)
                           for i in range(T // B)])



class Dep:
    __slots__ = ("sem", "val")

    def __init__(self, sem, val):
        self.sem = sem
        self.val = val


class Sem:
    def __init__(self, h):
        self.h = h
        self.n = 0

    def inc(self, inst, k=1):
        inst.then_inc(self.h, k)
        self.n += k
        return Dep(self, self.n)

    def now(self):
        return Dep(self, self.n)


def wait(eng, *deps):
    best = {}
    for d in deps:
        if d is None:
            continue
        k = id(d.sem)
        if k not in best or best[k].val < d.val:
            best[k] = d
    for d in best.values():
        if d.val > 0:
            eng.wait_ge(d.sem.h, d.val)


class Arena:
    def __init__(self, buf, cols):
        self.buf = buf
        self.cols = cols
        self.off = 0

    def alloc(self, n):
        import traceback
        if self.off + n > self.cols:
            frames = " | ".join(f"{t.name}:{t.lineno}" for t in traceback.extract_stack()[-5:-1])
            raise AssertionError(
                f"arena overflow {self.off}+{n}>{self.cols} at {frames}")
        o = self.off
        self.off += n
        return self.buf[:, o:o + n]

    def mark(self):
        return self.off

    def reset(self, m):
        self.off = m



def build_program(T=2048, blocks=("l0a", "l0f", "l1a", "l1f"), race_detect=True, dbg=0):
    nc = bass.Bass("TRN2", target_bir_lowering=False,
                   detect_race_conditions=race_detect)
    QB = min(512, T)
    NQB = T // QB
    KTB = QB // P
    NBLK = (2 * T) // QB
    NVT = T // P

    stk = contextlib.ExitStack()

    def dram(name, shape, dt, kind="ExternalInput", addr_space="Local"):
        return nc.dram_tensor(name, shape, dt, kind=kind, addr_space=addr_space).ap()

    def sbuf(name, shape, dt):
        return stk.enter_context(nc.sbuf_tensor(name, shape, dt))

    def mksem(name):
        return Sem(stk.enter_context(nc.semaphore(name)))

    x_in = dram("x_fm", [D, T], F32)
    y_out = dram("y_fm", [D, T], F32, kind="ExternalOutput")
    QBv = min(512, T)
    dbg_out = dram("dbg", [3, P, max(NCH * QBv, (QBv // P) * VW)], F32, kind="ExternalOutput") if dbg else None
    dbg2_out = dram("dbg2", [P, 16 * QBv], F32, kind="ExternalOutput") if dbg else None
    dbg3_out = dram("dbg3", [P, 6 * QBv + 2048], F32 if dbg in (4, 7) else BF16, kind="ExternalOutput") if dbg else None
    nonce_in = dram("nonce", [1, 1], I32)

    W = {}

    def win(name, shape):
        W[name] = dram(name, shape, F32)

    if "l0a" in blocks:
        win("l0a_wq", [16, P, P]); win("l0a_bq", [P, 2])
        win("l0a_wqc", [14, P, P]); win("l0a_bqc", [P, 7])
        win("l0a_wqr", [8, P, P]); win("l0a_bqr", [P, 1])
        win("l0a_wqrs", [8, P, P]); win("l0a_bqrs", [P, 1])
        win("l0a_wkv", [16, P, P]); win("l0a_bkv", [P, 2])
        win("l0a_wkr", [8, P, P]); win("l0a_bkr", [P, 1])
        win("l0a_wkrs", [8, P, P]); win("l0a_bkrs", [P, 1])
        win("l0a_wkc", [14, P, P]); win("l0a_bkc", [P, 7])
        win("l0a_wv", [4, P, 512]); win("l0a_bvb", [P, D])
        win("l0a_wo", [NH * NCH, HD, P]); win("l0a_bo", [P, 8])
        win("l0a_cos", [P, T]); win("l0a_sin", [P, T])
        win("l0a_mask", [2 * KTB, P, QB])

    if "l1a" in blocks:
        win("l1a_wq", [64, P, P]); win("l1a_bq", [P, 8])
        win("l1a_wqs", [64, P, P]); win("l1a_bqs", [P, 8])
        win("l1a_wk", [64, P, P]); win("l1a_bk", [P, 8])
        win("l1a_wks", [64, P, P]); win("l1a_bks", [P, 8])
        win("l1a_wv", [16, P, 512]); win("l1a_bvb", [P, D])
        win("l1a_cos", [8, P, T]); win("l1a_sin", [8, P, T])
    if "l1f" in blocks:
        win("l1f_router", [8, P, 16])
        win("l1f_gu", [16, 8, 2, 8, P, P]); win("l1f_wdp", [16, 4, 2, 8, P, P])
        win("l1f_sgu", [32, 2, 8, P, P])
        win("l1f_sd", [32, 8, P, P])
    if "l0f" in blocks:
        win("l0f_g1", [P, 8]); win("l0f_g2", [P, 8])
        win("l0f_wq", [64, P, P]); win("l0f_wk", [64, P, P])
        win("l0f_wv", [16, P, 512]); win("l0f_wo", [NH * NCH, HD, P])
        win("l0f_gate", [8, P, 8])
        win("l0f_w13", [8, 8, 4, 8, P, P])
        win("l0f_w2p", [8, 4, 2, 16, P, P])

    win("ident8", [8, 8, P])
    win("identt", [P, P])
    win("ident16", [16, 16, P])
    XPTS = [b for b in ("l0a", "l0f", "l1a") if b in blocks]
    ksh = [dram(f"ksh{i}", [2, P, NCH * T], BF16, kind="Internal", addr_space="Shared")
           for i in range(len(XPTS))]
    vsh = [dram(f"vsh{i}", [2, NVT, P, VW], BF16, kind="Internal", addr_space="Shared")
           for i in range(len(XPTS))]
    flg = dram("flg", [1, 2 * max(len(XPTS), 1)], I32, kind="Internal",
               addr_space="Shared")
    spill = dram("spillx", [P, NCH * T], F32, kind="Internal")

    X = sbuf("X", [P, NCH * T], F32)
    XBF = sbuf("XBF", [P, NCH * T], BF16)
    ABF_COLS = 32768
    AF32_COLS = 9216
    abf = Arena(sbuf("ABF", [P, ABF_COLS], BF16), ABF_COLS)
    af32 = Arena(sbuf("AF32", [P, AF32_COLS], F32), AF32_COLS)
    BIA = sbuf("BIA", [P, 64], F32)
    ROWS = sbuf("ROWSS", [P, 5 * 512], F32)
    ONEb = sbuf("ONEb", [P, P], BF16)
    ONEf = sbuf("ONEf", [P, P], F32)
    CST = sbuf("CST", [P, 16], F32)
    NON = sbuf("NON", [1, 1], I32)
    PFL = sbuf("PFL", [1, 1], I32)
    JNK = sbuf("JNK", [P, P], BF16)

    pb01_t = stk.enter_context(nc.psum_tensor("pb01", [P, 1024], F32))
    pb2345_t = stk.enter_context(nc.psum_tensor("pb2345", [P, 2048], F32))
    pb6_t = stk.enter_context(nc.psum_tensor("pb6", [P, 512], F32))
    pbT = stk.enter_context(nc.psum_tensor("pbT", [P, 1024], BF16))
    pb = ([pb01_t[:, 0:512], pb01_t[:, 512:1024]]
          + [pb2345_t[:, k * 512:(k + 1) * 512] for k in range(4)]
          + [pb6_t[:, 0:512]])
    pbdep = [None] * 8

    sp = mksem("sp")
    sa = mksem("sa")
    sv = mksem("sv")
    spoll = mksem("spoll")
    sready = mksem("sready")
    dpool_hw = [mksem(f"dph{i}") for i in range(16)]
    dpool_sw = [mksem(f"dps{i}") for i in range(20)]
    dpi = [0, 0]

    def dgrp(sw=False):
        if sw:
            s_ = dpool_sw[dpi[1] % len(dpool_sw)]
            dpi[1] += 1
        else:
            s_ = dpool_hw[dpi[0] % len(dpool_hw)]
            dpi[0] += 1
        return s_

    def emit(sy):
        t_ = nc.tensor
        v_ = nc.vector
        s_ = nc.scalar
        g_ = nc.gpsimd

        def Xc(c, t0=0, n=None):
            n = T if n is None else n
            return X[:, c * T + t0: c * T + t0 + n]

        def XBFc(c, t0=0, n=None):
            n = T if n is None else n
            return XBF[:, c * T + t0: c * T + t0 + n]

        pid = g_.partition_id()
        r_par = g_.alloc_register("par")
        r_cond = g_.alloc_register("cond")
        r_fv = g_.alloc_register("fv")
        r_tgt = g_.alloc_register("tgt")
        g_.reg_alu(r_par, pid, 1, ALU.bitwise_and)
        g_.reg_mov(r_tgt, 0)

        d_jnk = sv.inc(v_.memset(JNK[:], 0.5))
        v_.memset(ONEb[:], 1.0)
        d_one = sv.inc(v_.memset(ONEf[:], 1.0))
        cst_map = {}

        def cst(v):
            v = float(v)
            assert v in cst_map, f"constant {v} not preregistered"
            return CST[:, cst_map[v]:cst_map[v] + 1]

        for _cv in (-C_L0A, -C_L0F, -C_L1A, EPS, 1.0 / D, 0.125, 1.0 / 32.0, -1.0,
                    -1e9, 1.0):
            _cv = float(_cv)
            if _cv not in cst_map:
                cst_map[_cv] = len(cst_map)
                v_.memset(CST[:, cst_map[_cv]:cst_map[_cv] + 1], _cv)
        d_konst = sv.inc(v_.nop()) if False else sv.inc(
            v_.memset(CST[:, 15:16], 0.0))

        d_non = dgrp().inc(sy.dma_start(out=NON[:], in_=nonce_in), 16)
        d_x = dgrp().inc(sy.dma_start(out=X[:].rearrange("p (c t) -> p c t", t=T),
                                      in_=x_in.rearrange("(c p) t -> p c t", p=P)), 16)

        wait(t_, d_jnk)
        for _ in range(20):
            t_.matmul(pb[6][:, 0:P], JNK[:], JNK[:], start=True, stop=True)
        pbdep[6] = None

        wait(s_, d_x)
        d_xbf = sa.inc(s_.activation(XBF[:], X[:], AF.Copy))
        st = {"xbf": d_xbf}
        state = st

        def load_w(name, arena=abf):
            w = W[name]
            shp = list(w.tensor.shape)
            pr, cols = shp[-2], shp[-1]
            a = int(np.prod(shp[:-2])) if len(shp) > 2 else 1
            dst = arena.alloc(a * cols)
            if len(shp) == 2:
                src = w
                dstv = dst[0:pr, :]
            else:
                assert len(shp) == 3
                src = w.rearrange("a p c -> p a c")
                dstv = dst[0:pr, :].rearrange("p (a c) -> p a c", c=cols)
            gs = dgrp(sw=True)
            return dst, gs.inc(g_.dma_start(out=dstv, in_=src), 16)

        def load_f32(name, arena=af32):
            w = W[name]
            shp = list(w.tensor.shape)
            cols = shp[-1]
            a = int(np.prod(shp[:-2])) if len(shp) > 2 else 1
            dst = arena.alloc(a * cols)
            if len(shp) == 2:
                src = w
                dstv = dst
            else:
                assert len(shp) == 3
                src = w.rearrange("a p c -> p a c")
                dstv = dst.rearrange("p (a c) -> p a c", c=cols)
            gs = dgrp()
            return dst, gs.inc(sy.dma_start(out=dstv, in_=src), 16)

        def load_bia(names):
            cols = {}
            c0 = 0
            gs = dgrp()
            last = None
            for nm in names:
                n = W[nm].tensor.shape[-1]
                last = gs.inc(sy.dma_start(out=BIA[:, c0:c0 + n], in_=W[nm]), 16)
                cols[nm] = c0
                c0 += n
            assert c0 <= 64
            return (lambda nm, m=0: BIA[:, cols[nm] + m:cols[nm] + m + 1]), last

        def wt(dst, k, m, nm, w=P, pr=P):
            return dst[0:pr, (k * nm + m) * w:(k * nm + m) * w + w]

        def proj(bi, woff, nk, nm, m, rhs, deps, n=QB, pr=P):
            wait(t_, pbdep[bi], *deps)
            for k in range(nk):
                inst = t_.matmul(pb[bi][:, 0:n], wt(woff, k, m, nm, pr=pr), rhs[k],
                                 start=(k == 0), stop=(k == nk - 1))
            return sp.inc(inst)

        def evac_act(bi, out_ap, func=AF.Copy, bias=0.0, scale=1.0, dep=None,
                     war=None, rows=None, n=None):
            n = QB if n is None else n
            wait(s_, dep, war)
            src = pb[bi][:, 0:n] if rows is None else pb[bi][rows[0]:rows[1], 0:n]
            if func == AF.Copy and not isinstance(bias, float):
                func = AF.Identity
            inst = s_.activation(out_ap, src, func, bias=bias, scale=scale)
            d = sa.inc(inst)
            pbdep[bi] = d
            return d

        def block_barrier():
            for eng in (t_, v_, s_, g_, sy):
                wait(eng, Dep(sp, sp.n), Dep(sa, sa.n), Dep(sv, sv.n))

        def pair_dma(dst0, dst1, src, sem):
            with g_.If_eq(r_par, 0):
                inst = g_.dma_start(out=dst0, in_=src)
                inst.then_inc(sem.h, 16)
            with g_.Else():
                inst = g_.dma_start(out=dst1, in_=src)
                inst.then_inc(sem.h, 16)
            sem.n += 16
            return Dep(sem, sem.n)

        def xwrite_k(pt, src, tb, deps, sem):
            wait(g_, *deps)
            kv = ksh[pt]
            return pair_dma(
                kv[0].rearrange("p (c t) -> p c t", t=T)[:, :, tb * QB:(tb + 1) * QB],
                kv[1].rearrange("p (c t) -> p c t", t=T)[:, :, tb * QB:(tb + 1) * QB],
                src, sem)

        def xwrite_v(pt, src, tb, deps, sem):
            wait(g_, *deps)
            return pair_dma(
                vsh[pt][0][tb * KTB:(tb + 1) * KTB].rearrange("n p w -> p n w"),
                vsh[pt][1][tb * KTB:(tb + 1) * KTB].rearrange("n p w -> p n w"),
                src, sem)

        def xflag(pt, deps):
            wait(g_, *deps, d_non)
            return pair_dma(flg[0:1, 2 * pt:2 * pt + 1],
                            flg[0:1, 2 * pt + 1:2 * pt + 2], NON[:], dgrp(sw=True))

        def xpoll(pt):
            nv = g_.value_load(NON[0:1, 0:1])
            g_.reg_mov(r_cond, 1)
            with g_.While(r_cond):
                with g_.If_eq(r_par, 0):
                    g_.dma_start(out=PFL[:], in_=flg[0:1, 2 * pt + 1:2 * pt + 2]) \
                        .then_inc(spoll.h, 16)
                with g_.Else():
                    g_.dma_start(out=PFL[:], in_=flg[0:1, 2 * pt:2 * pt + 1]) \
                        .then_inc(spoll.h, 16)
                g_.reg_add(r_tgt, r_tgt, 16)
                g_.wait_ge(spoll.h, r_tgt)
                g_.reg_load(r_fv, PFL[0:1, 0:1])
                g_.reg_alu(r_cond, r_fv, nv, ALU.not_equal)
            sready.n += 1
            g_.sem_inc(sready.h, 1)
            return Dep(sready, sready.n)


        def attention_mh(pt, causal, cshift, wo_load, bo_ap, qs_make, masks,
                         krd, vrd, et, etb, bcev, hva, scr2, qstate,
                         d_start=None, x_guard=None):
            krd_dep = [None, None]
            krd_free = [d_start, d_start]
            krd_sem = [dgrp(), dgrp()]
            scr_free = [None, None]
            wo_use = [None, None]
            wo_sem = [dgrp(), dgrp()]
            hva_free = None

            def stream(g2, sl, ch, h0):
                wait(sy, krd_free[sl])
                i1 = sy.dma_start(
                    out=krd[sl],
                    in_=ksh[pt][g2 % 2][:, ch * T + (g2 // 2) * QB:
                                        ch * T + (g2 // 2) * QB + QB])
                krd_sem[sl].inc(i1, 16)
                i2 = sy.dma_start(
                    out=vrd[sl].rearrange("p (n w) -> p n w", w=130),
                    in_=vsh[pt][g2 % 2][(g2 // 2) * KTB:(g2 // 2 + 1) * KTB,
                                        :, 65 * h0:65 * h0 + 130]
                    .rearrange("n p w -> p n w"))
                krd_dep[sl] = krd_sem[sl].inc(i2, 16)

            for i in range(NQB):
                ngrp = 2 * i + 2 if causal else NBLK
                nkt = ngrp * KTB
                qs, d_q = qs_make(i)
                hv_dep = [None] * NH
                prev_hv = None
                for sweep in range(NH // 2):
                    ch = sweep
                    heads = (2 * sweep, 2 * sweep + 1)
                    et_dep = {}
                    inst = None
                    for g2 in range(ngrp):
                        sl = g2 % 2
                        stream(g2, sl, ch, heads[0])
                        wait(t_, krd_dep[sl], d_q)
                        if dbg == 5 and i == 0 and sweep == 0 and g2 <= 1:
                            dgrp().inc(sy.dma_start(
                                out=dbg3_out[:, g2 * QB:(g2 + 1) * QB], in_=krd[sl]), 16)
                            dgrp().inc(sy.dma_start(
                                out=dbg3_out[:, 2 * QB + g2 * KTB * 130:
                                             2 * QB + (g2 + 1) * KTB * 130],
                                in_=vrd[sl]), 16)
                        for kt0 in range(0, KTB, 2):
                            kts = list(range(kt0, min(kt0 + 2, KTB)))
                            g0 = g2 * KTB + kt0
                            w_extra = ([pbdep[2], pbdep[3], pbdep[4], pbdep[5]]
                                       if g0 == 0 else [])
                            wait(t_, et_dep.get(g0 - 2),
                                 et_dep.get(g0 + len(kts) - 3), *w_extra)
                            d_sc = {}
                            for kt in kts:
                                gkt = g2 * KTB + kt
                                sb = 2 + 2 * (gkt % 2)
                                for hf in range(2):
                                    inst = t_.matmul(
                                        pb[sb + hf][:, 0:QB],
                                        krd[sl][64 * hf:64 * hf + 64,
                                                kt * P:kt * P + P],
                                        qs[64 * hf:64 * hf + 64,
                                           ch * QB:(ch + 1) * QB],
                                        start=True, stop=True)
                                d_sc[gkt] = sp.inc(inst)
                            wait(s_, d_sc[g0 + len(kts) - 1],
                                 prev_hv if g0 <= 1 else None)
                            inst = s_.activation(
                                etb.rearrange("p (k q) -> p k q", q=QB)
                                [:, 0:len(kts) * 2, :],
                                pb2345_t[:, 0:2048]
                                .rearrange("p (k q) -> p k q", q=512)
                                [:, 0:len(kts) * 2, 0:QB],
                                AF.Exp, bias=cst(-cshift), scale=cst(0.125))
                            d_e = sa.inc(inst)
                            for b in range(2, 6):
                                pbdep[b] = d_e
                            d_es_all = [d_e]
                            for kt in kts:
                                gkt = g2 * KTB + kt
                                dd = d_e
                                if causal and gkt >= (2 * i) * KTB:
                                    j = gkt - (2 * i) * KTB
                                    wait(v_, d_e)
                                    for hf in range(2):
                                        eslot = et[(gkt % 2) * 2 + hf]
                                        inst = v_.tensor_tensor(eslot, eslot,
                                                                masks[j], ALU.mult)
                                    dd = sv.inc(inst)
                                    d_es_all.append(dd)
                                et_dep[gkt] = dd
                            wait(t_, *d_es_all)
                            for kt in kts:
                                gkt = g2 * KTB + kt
                                if gkt == 0:
                                    wait(t_, pbdep[0], pbdep[1])
                                for hf in range(2):
                                    inst = t_.matmul(
                                        pb[hf][0:65, 0:QB],
                                        vrd[sl][:, kt * 130 + 65 * hf:
                                                kt * 130 + 65 * hf + 65],
                                        et[(gkt % 2) * 2 + hf],
                                        start=(gkt == 0), stop=(gkt == nkt - 1))
                        krd_free[sl] = sp.inc(inst)
                    d_po = krd_free[(ngrp - 1) % 2]
                    if dbg == 4 and i == 0 and sweep == 0:
                        dsb = st["dbg3_sb"]
                        wait(v_, d_po)
                        dv0 = sv.inc(v_.tensor_copy(dsb[0:65, 0:QB], pb[0][0:65, 0:QB]))
                        dv1 = sv.inc(v_.tensor_copy(dsb[0:65, QB:2 * QB], pb[1][0:65, 0:QB]))
                        wait(g_, dv0, dv1)
                        dgrp(sw=True).inc(g_.dma_start(out=dbg3_out[0:65, 0:2 * QB], in_=dsb[0:65, 0:2 * QB]), 16)
                    bcev2 = et[2]
                    wait(v_, d_po)
                    v_.reciprocal(ROWS[64:65, 0:QB], pb[0][64:65, 0:QB])
                    inst = v_.reciprocal(ROWS[64:65, QB:2 * QB],
                                         pb[1][64:65, 0:QB])
                    v_.drain()
                    d_r = sv.inc(inst)
                    wait(t_, d_r, pbdep[6], pbdep[2], d_one)
                    inst = t_.matmul(pb[6][0:64, 0:QB], ONEf[64:65, 0:64],
                                     ROWS[64:65, 0:QB], start=True, stop=True)
                    d_b0 = sp.inc(inst)
                    inst = t_.matmul(pb[2][0:64, 0:QB], ONEf[64:65, 0:64],
                                     ROWS[64:65, QB:2 * QB], start=True, stop=True)
                    d_b1 = sp.inc(inst)
                    d_bc0 = evac_act(6, bcev[0:64, :], dep=d_b0, rows=(0, 64))
                    d_bc1 = evac_act(2, bcev2[0:64, :], dep=d_b1, rows=(0, 64))
                    for hf in range(2):
                        bce = bcev if hf == 0 else bcev2
                        wait(v_, d_bc0 if hf == 0 else d_bc1)
                        inst = v_.tensor_tensor(
                            hva[0:64, heads[hf] * QB:(heads[hf] + 1) * QB],
                            pb[hf][0:64, 0:QB], bce[0:64, :], ALU.mult)
                        d_hv = sv.inc(inst)
                        pbdep[hf] = d_hv
                        hv_dep[heads[hf]] = d_hv
                        prev_hv = d_hv
                for m in range(NCH):
                    bi = 2 + (m % 2)
                    wo_ap, d_wo = wo_load(m, wo_use[m % 2])
                    wait(t_, pbdep[bi], d_wo, *hv_dep)
                    for h in range(NH):
                        inst = t_.matmul(pb[bi][:, 0:QB],
                                         wo_ap[0:64, h * P:h * P + P],
                                         hva[0:64, h * QB:(h + 1) * QB],
                                         start=(h == 0), stop=(h == NH - 1))
                    d_o = sp.inc(inst)
                    wo_use[m % 2] = d_o
                    sl2 = m % 2
                    d_ev = evac_act(bi, scr2[:, sl2 * QB:(sl2 + 1) * QB],
                                    bias=(bo_ap(m) if bo_ap else 0.0),
                                    dep=d_o, war=scr_free[sl2])
                    if dbg == 7 and i == 0 and m == 0:
                        wait(g_, d_ev)
                        dgrp(sw=True).inc(g_.dma_start(out=dbg3_out[:, 0:QB],
                                                in_=scr2[:, 0:QB]), 16)
                        dgrp(sw=True).inc(g_.dma_start(out=dbg3_out[0:HD, QB:QB + NH * P],
                                                in_=wo_ap[0:HD, :]), 16)
                    wait(v_, d_ev, x_guard if i == 0 else None)
                    inst = v_.tensor_tensor(Xc(m, i * QB, QB), Xc(m, i * QB, QB),
                                            scr2[:, sl2 * QB:(sl2 + 1) * QB], ALU.add)
                    scr_free[sl2] = sv.inc(inst)
                if dbg in (3, 7) and i == 0:
                    wait(g_, *hv_dep)
                    dgrp(sw=True).inc(g_.dma_start(out=dbg2_out[0:64, 0:NH * QB],
                                            in_=hva[0:64, 0:NH * QB]), 16)
                hva_free = sp.now()
                qstate["free"] = sp.now()
            return sv.now()

        def postln(gamma_ap=None, rms=False, scb=None, scrl=None):
            d_prev = sv.now()
            wait(s_, d_prev)
            d_cast = sa.inc(s_.activation(XBF[:], X[:], AF.Copy))
            d_apply = None
            R = ROWS
            for tb in range(NQB):
                wait(t_, d_cast, pbdep[6])
                for c in range(NCH):
                    inst = t_.matmul(pb[6][0:1, 0:QB], ONEb[:, 0:1],
                                     XBFc(c, tb * QB, QB),
                                     start=(c == 0), stop=(c == NCH - 1))
                d_s1 = sp.inc(inst)
                d_sq = []
                for c in range(NCH):
                    wait(s_, d_cast)
                    inst = s_.activation(scb[:, c * QB:(c + 1) * QB],
                                         XBFc(c, tb * QB, QB), AF.Square)
                    d_sq.append(sa.inc(inst))
                wait(t_, pbdep[5], *d_sq)
                for c in range(NCH):
                    inst = t_.matmul(pb[5][0:1, 0:QB], ONEb[:, 0:1],
                                     scb[:, c * QB:(c + 1) * QB],
                                     start=(c == 0), stop=(c == NCH - 1))
                d_s2 = sp.inc(inst)
                r_mean = R[0:1, 0:QB]
                r_ms = R[0:1, QB:2 * QB]
                r_tmp = R[0:1, 2 * QB:3 * QB]
                r_rstd = R[0:1, 3 * QB:4 * QB]
                r_mneg = R[0:1, 4 * QB:5 * QB]
                wait(s_, d_s1, d_s2)
                s_.activation(r_mean, pb[6][0:1, 0:QB], AF.Copy, scale=1.0 / D)
                inst = s_.activation(r_ms, pb[5][0:1, 0:QB], AF.Copy, scale=1.0 / D)
                s_.drain()
                d_st = sa.inc(inst)
                pbdep[6] = d_st
                pbdep[5] = d_st
                if rms:
                    wait(s_, d_st)
                    inst = s_.activation(r_tmp, r_ms, AF.Sqrt, bias=cst(EPS)[0:1, :])
                    s_.drain()
                    d_sd2 = sa.inc(inst)
                    wait(v_, d_sd2)
                    inst = v_.reciprocal(r_rstd, r_tmp)
                    v_.drain()
                    d_rs = sv.inc(inst)
                else:
                    wait(v_, d_st)
                    v_.tensor_tensor(r_tmp, r_mean, r_mean, ALU.mult)
                    v_.drain()
                    inst = v_.tensor_tensor(r_tmp, r_ms, r_tmp, ALU.subtract)
                    v_.drain()
                    d_var = sv.inc(inst)
                    wait(s_, d_var)
                    inst = s_.activation(r_ms, r_tmp, AF.Sqrt, bias=cst(EPS)[0:1, :])
                    s_.drain()
                    d_sd2 = sa.inc(inst)
                    wait(v_, d_sd2)
                    v_.reciprocal(r_rstd, r_ms)
                    v_.drain()
                    v_.tensor_tensor(r_mneg, r_mean, r_rstd, ALU.mult)
                    v_.drain()
                    inst = v_.tensor_scalar(r_mneg, r_mneg, cst(-1.0)[0:1, :], None, ALU.mult)
                    v_.drain()
                    d_rs = sv.inc(inst)
                wait(t_, d_rs, pbdep[6], pbdep[7], d_one)
                inst = t_.matmul(pb[6][:, 0:QB], ONEf[0:1, 0:P], r_rstd,
                                 start=True, stop=True)
                if not rms:
                    inst = t_.matmul(pb[5][:, 0:QB], ONEf[0:1, 0:P], r_mneg,
                                     start=True, stop=True)
                d_b = sp.inc(inst)
                for c in range(NCH):
                    wait(v_, d_b)
                    if rms:
                        inst = v_.tensor_tensor(scrl[:, 0:QB], Xc(c, tb * QB, QB),
                                                pb[6][:, 0:QB], ALU.mult)
                        sv.inc(inst)
                        v_.drain()
                        inst = v_.tensor_scalar(XBFc(c, tb * QB, QB), scrl[:, 0:QB],
                                                gamma_ap(c), None, ALU.mult)
                        d_apply = sv.inc(inst)
                    else:
                        inst = v_.tensor_tensor(scrl[:, 0:QB], Xc(c, tb * QB, QB),
                                                pb[6][:, 0:QB], ALU.mult)
                        sv.inc(inst)
                        v_.drain()
                        inst = v_.tensor_tensor(Xc(c, tb * QB, QB), scrl[:, 0:QB],
                                                pb[5][:, 0:QB], ALU.add)
                        d_apply = sv.inc(inst)
                pbdep[6] = d_apply
                pbdep[5] = d_apply
            if not rms:
                wait(s_, d_apply)
                d_apply = sa.inc(s_.activation(XBF[:], X[:], AF.Copy))
            return d_apply

        def l0_attn():
            pt = XPTS.index("l0a")
            m0, mf = abf.mark(), af32.mark()
            bia_ap, d_b1 = load_bia(["l0a_bq", "l0a_bqc", "l0a_bqr", "l0a_bqrs",
                                     "l0a_bkv", "l0a_bkr", "l0a_bkrs", "l0a_bkc",
                                     "l0a_bo"])
            rtc, d_b2 = load_f32("l0a_cos")
            rts, d_b3 = load_f32("l0a_sin")
            bvb, d_b4 = load_f32("l0a_bvb")
            d_aux = [d_b1, d_b2, d_b3, d_b4]
            ckv = abf.alloc(2 * QB * 2)
            scr = af32.alloc(5 * QB)
            kx_sem = [dgrp(sw=True), dgrp(sw=True)]
            vx_sem = [dgrp(sw=True), dgrp(sw=True)]
            d_xbf = st["xbf"]
            ck_free = [None, None]
            st_free = [None, None]
            vst_free = [None, None]

            mA = abf.mark()
            wkv_w, dw1 = load_w("l0a_wkv")
            wkr_w, dw2 = load_w("l0a_wkr")
            wkrs_w, dw3 = load_w("l0a_wkrs")
            wkc_w, dw4 = load_w("l0a_wkc")
            wv_w, dw5 = load_w("l0a_wv")
            kst = abf.alloc(NCH * QB * 2)
            vst = abf.alloc(KTB * VW * 2)
            d_vini = sv.inc(v_.memset(vst[:], 1.0))
            wq_w = wqr_w = wqrs_w = wqc_w = None
            wait(s_, *d_aux)
            wait(v_, *d_aux, d_konst)
            wait(t_, dw1, dw2, dw3, dw4, dw5, d_xbf)
            d_w = None

            def qk_path(tb, is_k, out8, slot_war):
                r = tb % 2
                ck = ckv[:, r * 2 * QB:(r + 1) * 2 * QB]
                rhs = [XBFc(c, tb * QB, QB) for c in range(NCH)]
                wmain = wkv_w if is_k else wq_w
                wrot = wkr_w if is_k else wqr_w
                wrots = wkrs_w if is_k else wqrs_w
                wcc = wkc_w if is_k else wqc_w
                bm = "l0a_bkv" if is_k else "l0a_bq"
                br = "l0a_bkr" if is_k else "l0a_bqr"
                brs = "l0a_bkrs" if is_k else "l0a_bqrs"
                bc = "l0a_bkc" if is_k else "l0a_bqc"
                d_ck = []
                for m in range(2):
                    dmm = proj(m, wmain, 8, 2, m, rhs, [d_w, ck_free[r]])
                    d_ck.append(evac_act(m, ck[:, m * QB:(m + 1) * QB], AF.Gelu,
                                         bias=bia_ap(bm, m), dep=dmm))
                dmm = proj(2, wrot, 8, 1, 0, rhs, [d_w])
                d_kr = evac_act(2, scr[:, 0 * QB:1 * QB], AF.Copy,
                                bias=bia_ap(br), dep=dmm)
                dmm = proj(3, wrots, 8, 1, 0, rhs, [d_w])
                d_krs = evac_act(3, scr[:, 1 * QB:2 * QB], AF.Copy,
                                 bias=bia_ap(brs), dep=dmm)
                wait(v_, d_kr, d_krs)
                v_.tensor_tensor(scr[:, 2 * QB:3 * QB], scr[:, 0 * QB:1 * QB],
                                 rtc[:, tb * QB:(tb + 1) * QB], ALU.mult)
                v_.tensor_tensor(scr[:, 3 * QB:4 * QB], scr[:, 1 * QB:2 * QB],
                                 rts[:, tb * QB:(tb + 1) * QB], ALU.mult)
                v_.drain()
                inst = v_.tensor_tensor(scr[:, 4 * QB:5 * QB], scr[:, 2 * QB:3 * QB],
                                        scr[:, 3 * QB:4 * QB], ALU.add)
                d_rope = sv.inc(inst)
                for m in range(7):
                    bi = 4 + (m % 2)
                    dmm = proj(bi, wcc, 2, 7, m, [ck[:, 0:QB], ck[:, QB:2 * QB]],
                               [d_w] + d_ck)
                    evac_act(bi, out8[:, m * QB:(m + 1) * QB], AF.Gelu,
                             bias=bia_ap(bc, m), dep=dmm, war=slot_war)
                wait(s_, d_rope, slot_war)
                inst = s_.activation(out8[:, 7 * QB:8 * QB], scr[:, 4 * QB:5 * QB],
                                     AF.Gelu)
                d_last = sa.inc(inst)
                return ck, d_ck, d_last

            xw = []
            for tb in range(NQB):
                r = tb % 2
                kst_r = kst[:, r * NCH * QB:(r + 1) * NCH * QB]
                vst_r = vst[:, r * KTB * VW:(r + 1) * KTB * VW]
                ck, d_ck, d_kst = qk_path(tb, True, kst_r, st_free[r])
                d_v = []
                for n in range(2):
                    bi = 5 + n
                    for tt in range(KTB):
                        wait(t_, pbdep[bi], d_ck[0], d_ck[1])
                        for kc in range(2):
                            inst = t_.matmul(
                                pb[bi][:, 0:512],
                                ck[:, kc * QB + tt * P:kc * QB + tt * P + P],
                                wv_w[:, (kc * 2 + n) * 512:(kc * 2 + n) * 512 + 512],
                                start=(kc == 0), stop=(kc == 1))
                        d_mmv = sp.inc(inst)
                        wait(v_, d_mmv, d_vini, vst_free[r])
                        dst = vst_r[:, tt * VW + n * 520: tt * VW + n * 520 + 520] \
                            .rearrange("p (h w) -> p h w", w=65)[:, :, 0:64]
                        inst = v_.tensor_tensor(
                            dst,
                            pb[bi][:, 0:512].rearrange("p (h w) -> p h w", w=64),
                            bvb[:, n * 512:(n + 1) * 512]
                            .rearrange("p (h w) -> p h w", w=64),
                            ALU.add)
                        d_vv = sv.inc(inst)
                        pbdep[bi] = d_vv
                        d_v.append(d_vv)
                if dbg and tb == 0:
                    wait(g_, d_kst)
                    dgrp(sw=True).inc(g_.dma_start(out=dbg_out[0][:, 0:NCH * QB], in_=kst_r), 16)
                    wait(g_, *d_v)
                    dgrp(sw=True).inc(g_.dma_start(
                        out=dbg_out[1][:, 0:KTB * VW], in_=vst_r), 16)
                dk = xwrite_k(pt, kst_r.rearrange("p (c t) -> p c t", t=QB), tb,
                              [d_kst], kx_sem[r])
                dv = xwrite_v(pt, vst_r.rearrange("p (n w) -> p n w", w=VW), tb,
                              d_v, vx_sem[r])
                xw += [dk, dv]
                st_free[r] = dk
                vst_free[r] = dv
                ck_free[r] = sp.now()
            xflag(pt, xw)
            d_ready = xpoll(pt)

            abf.reset(mA)
            wq_w, dq1 = load_w("l0a_wq")
            wqr_w, dq2 = load_w("l0a_wqr")
            wqrs_w, dq3 = load_w("l0a_wqrs")
            wqc_w, dq4 = load_w("l0a_wqc")
            d_wq = [dq1, dq2, dq3, dq4]
            wait(t_, *d_wq)
            mask_t = scb_ln
            d_mask = dgrp(sw=True).inc(
                g_.dma_start(out=mask_t[:, 0:2 * KTB * QB]
                             .rearrange("p (a q) -> p a q", q=QB),
                             in_=W["l0a_mask"].rearrange("a p q -> p a q")), 16)
            masks = [mask_t[:, j * QB:(j + 1) * QB] for j in range(2 * KTB)]
            wait(v_, d_mask)
            qsb = abf.alloc(NCH * QB)
            krd = [abf.alloc(QB) for _ in range(2)]
            vrd = [abf.alloc(KTB * 130) for _ in range(2)]
            etb = abf.alloc(4 * QB)
            et = [etb[:, k * QB:(k + 1) * QB] for k in range(4)]
            bcev = et[3]
            st['dbg3_sb2'] = abf.alloc(4 * QB) if dbg == 6 else None
            st["dbg3_sb"] = af32.alloc(2 * QB) if dbg == 4 else None
            hva = abf.alloc(NH * QB)
            wo_ring = [abf.alloc(NH * P) for _ in range(2)]
            scr2 = af32.alloc(2 * QB)
            d_xchg = xw[-1]
            qstate = {"free": None}
            wait(sy, d_ready)

            def qs_make(i):
                _, _, d_last = qk_path(i, False, qsb, qstate["free"])
                if dbg and i == 0:
                    wait(g_, d_last)
                    dgrp(sw=True).inc(g_.dma_start(out=dbg_out[2][:, 0:NCH * QB], in_=qsb), 16)
                return qsb, d_last

            wo_sems = [dgrp(sw=True), dgrp(sw=True)]

            def wo_load(m, prev_use):
                sl = m % 2
                wait(g_, prev_use)
                inst = g_.dma_start(
                    out=wo_ring[sl][0:HD, :].rearrange("p (a c) -> p a c", c=P),
                    in_=W["l0a_wo"][m * NH:(m + 1) * NH].rearrange("a p c -> p a c"))
                return wo_ring[sl], wo_sems[sl].inc(inst, 16)

            attention_mh(pt, True, C_L0A, wo_load, lambda m: bia_ap("l0a_bo", m),
                         qs_make, masks, krd, vrd, et, etb, bcev, hva, scr2,
                         qstate, d_start=d_xchg)
            abf.reset(m0)
            af32.reset(mf)

        def l0_ffn():
            block_barrier()
            pt = XPTS.index("l0f")
            m0, mf = abf.mark(), af32.mark()
            gam_ap, d_g = load_bia(["l0f_g1", "l0f_g2"])
            wait(v_, d_g)
            d_rms1 = postln(gamma_ap=lambda c: gam_ap("l0f_g1", c), rms=True,
                            scb=scb_ln, scrl=scrl_ln)
            wait(g_, sv.now())
            d_spill = dgrp(sw=True).inc(g_.dma_start(out=spill, in_=X[:]), 16)

            mA = abf.mark()
            wv_w, dk2 = load_w("l0f_wv")
            kst = abf.alloc(NCH * QB)
            vst = abf.alloc(KTB * VW * 2)
            pring = [abf.alloc(8 * P) for _ in range(2)]
            pr_sems = [dgrp(sw=True), dgrp(sw=True)]
            pr_free = [None, None]
            d_vini = sv.inc(v_.memset(vst[:], 1.0))
            wait(t_, dk2, d_rms1)
            kx_sem = [dgrp(sw=True), dgrp(sw=True)]
            vx_sem = [dgrp(sw=True), dgrp(sw=True)]
            st_free = [None, None]
            vst_free = [None, None]
            xw = []

            def plain_path(tb, wname, out8, slot_war):
                rhs = [XBFc(c, tb * QB, QB) for c in range(NCH)]
                d_last = None
                for m in range(NCH):
                    wsl = m % 2
                    wait(g_, pr_free[wsl])
                    inst = g_.dma_start(
                        out=pring[wsl].rearrange("p (a c) -> p a c", c=P),
                        in_=W[wname][m * 8:(m + 1) * 8].rearrange("a p c -> p a c"))
                    d_w = pr_sems[wsl].inc(inst, 16)
                    bi = m % 4
                    dmm = proj(bi, pring[wsl], 8, 1, 0, rhs, [d_w])
                    pr_free[wsl] = dmm
                    d_last = evac_act(bi, out8[:, m * QB:(m + 1) * QB], AF.Copy,
                                      dep=dmm, war=slot_war)
                return d_last

            for tb in range(NQB):
                r = tb % 2
                kst_r = kst
                vst_r = vst[:, r * KTB * VW:(r + 1) * KTB * VW]
                d_kst = plain_path(tb, "l0f_wk", kst_r, st_free[0])
                d_v = []
                for n in range(2):
                    bi = 4 + n
                    for tt in range(KTB):
                        wait(t_, pbdep[bi])
                        for kc in range(NCH):
                            inst = t_.matmul(
                                pb[bi][:, 0:512],
                                XBFc(kc, tb * QB + tt * P, P),
                                wv_w[:, (kc * 2 + n) * 512:(kc * 2 + n) * 512 + 512],
                                start=(kc == 0), stop=(kc == NCH - 1))
                        d_mmv = sp.inc(inst)
                        wait(v_, d_mmv, d_vini, vst_free[r])
                        dst = vst_r[:, tt * VW + n * 520: tt * VW + n * 520 + 520] \
                            .rearrange("p (h w) -> p h w", w=65)[:, :, 0:64]
                        inst = v_.tensor_copy(
                            dst, pb[bi][:, 0:512].rearrange("p (h w) -> p h w", w=64))
                        d_vv = sv.inc(inst)
                        pbdep[bi] = d_vv
                        d_v.append(d_vv)
                dk = xwrite_k(pt, kst_r.rearrange("p (c t) -> p c t", t=QB), tb,
                              [d_kst], kx_sem[r])
                dv = xwrite_v(pt, vst_r.rearrange("p (n w) -> p n w", w=VW), tb,
                              d_v, vx_sem[r])
                xw += [dk, dv]
                st_free[r] = dk
                vst_free[r] = dv
            xflag(pt, xw)
            d_ready = xpoll(pt)

            abf.reset(mA)
            pring = [abf.alloc(8 * P) for _ in range(2)]
            pr_free = [None, None]
            qsb = abf.alloc(NCH * QB)
            krd = [abf.alloc(QB) for _ in range(2)]
            vrd = [abf.alloc(KTB * 130) for _ in range(2)]
            etb = abf.alloc(4 * QB)
            et = [etb[:, k * QB:(k + 1) * QB] for k in range(4)]
            bcev = et[3]
            hva = abf.alloc(NH * QB)
            wo_ring = [abf.alloc(NH * P) for _ in range(2)]
            scr2 = af32.alloc(2 * QB)
            d_xchg = xw[-1]
            qstate = {"free": None}
            wait(sy, d_ready)
            wo_sems = [dgrp(sw=True), dgrp(sw=True)]

            def qs_make(i):
                d_last = plain_path(i, "l0f_wq", qsb, qstate["free"])
                return qsb, d_last

            def wo_load(m, prev_use):
                sl = m % 2
                wait(g_, prev_use)
                inst = g_.dma_start(
                    out=wo_ring[sl][0:HD, :].rearrange("p (a c) -> p a c", c=P),
                    in_=W["l0f_wo"][m * NH:(m + 1) * NH].rearrange("a p c -> p a c"))
                return wo_ring[sl], wo_sems[sl].inc(inst, 16)

            attention_mh(pt, False, C_L0F, wo_load, None, qs_make, None,
                         krd, vrd, et, etb, bcev, hva, scr2, qstate,
                         d_start=d_xchg, x_guard=d_spill)
            abf.reset(mA)

            d_xn2 = postln(gamma_ap=lambda c: gam_ap("l0f_g2", c), rms=True,
                           scb=scb_ln, scrl=scrl_ln)
            wait(t_, d_xn2)

            oh8_sb, d_oh = load_w("ident8")
            idt_sb, d_id = load_w("identt")
            gate_w, d_gt = load_w("l0f_gate")
            weT = abf.alloc(T)
            bcw = [abf.alloc(QB) for _ in range(2)]
            sgb = abf.alloc(QB * 2)
            usb = abf.alloc(QB * 2)
            hsb = abf.alloc(24 * QB)
            wring = [abf.alloc(32 * P) for _ in range(2)]
            webf = abf.alloc(16)

            def hsb_ap(j, fc):
                s = j * 16 + fc
                return (scb_ln[:, s * QB:(s + 1) * QB] if s < 8
                        else hsb[:, (s - 8) * QB:(s - 7) * QB])
            rsc = af32.alloc(64)
            spl_f = af32.alloc(2 * QB)
            wr_sems = [dgrp(sw=True), dgrp(sw=True)]
            wr_free = [None, None]
            spl_sems = [dgrp(), dgrp()]
            wait(t_, d_oh, d_id, d_gt)

            lg_free = None
            d_weT = None
            for tt in range(T // P):
                bi = tt % 2
                wait(t_, pbdep[bi])
                for kc in range(NCH):
                    inst = t_.matmul(pb[bi][:, 0:8],
                                     XBFc(kc, tt * P, P),
                                     gate_w[:, kc * 8:(kc + 1) * 8],
                                     start=(kc == 0), stop=(kc == NCH - 1))
                d_lg = sp.inc(inst)
                lg = rsc[:, 0:8]
                d_ev = evac_act(bi, lg, dep=d_lg, war=lg_free, n=8)
                m1 = rsc[:, 8:9]
                z2 = rsc[:, 16:24]
                m2 = rsc[:, 9:10]
                e12 = rsc[:, 10:11]
                p1 = rsc[:, 11:12]
                p2 = rsc[:, 12:13]
                eq1 = rsc[:, 24:32]
                eq2 = rsc[:, 32:40]
                wef = rsc[:, 40:48]
                wait(v_, d_ev)
                v_.tensor_reduce(m1, lg, mybir.AxisListType.X, ALU.max)
                v_.drain()
                v_.tensor_scalar(eq1, lg, m1, None, ALU.is_ge)
                v_.drain()
                v_.tensor_scalar(z2, eq1, cst(-1e9), None, ALU.mult)
                v_.drain()
                v_.tensor_tensor(z2, lg, z2, ALU.add)
                v_.drain()
                v_.tensor_reduce(m2, z2, mybir.AxisListType.X, ALU.max)
                v_.drain()
                inst = v_.tensor_tensor(e12, m2, m1, ALU.subtract)
                v_.drain()
                d_d = sv.inc(inst)
                wait(s_, d_d)
                inst = s_.activation(e12, e12, AF.Exp)
                s_.drain()
                d_e = sa.inc(inst)
                wait(v_, d_e)
                v_.tensor_scalar(p2, e12, cst(1.0), None, ALU.add)
                v_.drain()
                v_.reciprocal(p1, p2)
                v_.drain()
                v_.tensor_tensor(p2, e12, p1, ALU.mult)
                v_.drain()
                v_.tensor_scalar(eq2, z2, m2, None, ALU.is_ge)
                v_.drain()
                v_.tensor_scalar(eq1, eq1, p1, None, ALU.mult)
                v_.tensor_scalar(eq2, eq2, p2, None, ALU.mult)
                v_.drain()
                v_.tensor_tensor(wef, eq1, eq2, ALU.add)
                v_.drain()
                inst = v_.tensor_copy(webf[:, 0:8], wef)
                v_.drain()
                d_web = sv.inc(inst)
                lg_free = d_web
                wait(t_, d_web, pbdep[7])
                inst = t_.transpose(pbT[0:8, 0:P], webf[:, 0:8], idt_sb)
                d_tr = sp.inc(inst)
                wait(s_, d_tr)
                inst = s_.activation(weT[0:8, tt * P:(tt + 1) * P], pbT[0:8, 0:P],
                                     AF.Copy)
                d_weT = sa.inc(inst)
                pbdep[7] = d_weT

            bc_free = [None, None]
            h_free = None
            sg_free = [None, None]
            us_free = [None, None]
            for p0 in range(0, NQB, 2):
                tbs = tuple(range(p0, min(p0 + 2, NQB)))
                rhs2 = [[XBFc(c, tb * QB, QB) for c in range(NCH)] for tb in tbs]
                for e in range(8):
                    d_bce = []
                    for j, tb in enumerate(tbs):
                        wait(t_, pbdep[6], d_weT, bc_free[j])
                        inst = t_.matmul(pb[6][:, 0:QB],
                                         oh8_sb[0:8, e * P:(e + 1) * P],
                                         weT[0:8, tb * QB:(tb + 1) * QB],
                                         start=True, stop=True)
                        d_bc = sp.inc(inst)
                        d_bce.append(evac_act(6, bcw[j], dep=d_bc,
                                              war=bc_free[j]))
                    d_h = [[], []]
                    d_us_last = [None, None]
                    for fcp in range(8):
                        wsl = fcp % 2
                        wait(g_, wr_free[wsl])
                        inst = g_.dma_start(
                            out=wring[wsl][:, 0:32 * P]
                            .rearrange("p (x c) -> p x c", c=P),
                            in_=W["l0f_w13"][e, fcp]
                            .rearrange("s a p c -> p (s a) c"))
                        d_w = wr_sems[wsl].inc(inst, 16)
                        d_mm_last = None
                        for fi in range(2):
                            fc = 2 * fcp + fi
                            for j, tb in enumerate(tbs):
                                bi = j
                                bj = 2 + j
                                dmm = proj(bi, wring[wsl][:, (2 * fi) * 8 * P:
                                                          (2 * fi + 1) * 8 * P],
                                           8, 1, 0, rhs2[j],
                                           [d_w] if fi + j == 0 else [])
                                d_sg = evac_act(bi, sgb[:, j * QB:(j + 1) * QB],
                                                AF.Silu, dep=dmm, war=sg_free[j])
                                dmm2 = proj(bj, wring[wsl][:, (2 * fi + 1) * 8 * P:
                                                           (2 * fi + 2) * 8 * P],
                                            8, 1, 0, rhs2[j], [])
                                d_mm_last = dmm2
                                wait(v_, dmm2, d_bce[j], d_sg, h_free, us_free[j])
                                inst = v_.tensor_tensor(
                                    usb[:, j * QB:(j + 1) * QB],
                                    pb[bj][:, 0:QB], bcw[j], ALU.mult)
                                d_us = sv.inc(inst)
                                pbdep[bj] = d_us
                                d_us_last[j] = d_us
                                v_.drain()
                                inst = v_.tensor_tensor(
                                    hsb_ap(j, fc), sgb[:, j * QB:(j + 1) * QB],
                                    usb[:, j * QB:(j + 1) * QB], ALU.mult)
                                dh = sv.inc(inst)
                                d_h[j].append(dh)
                                sg_free[j] = dh
                                us_free[j] = dh
                        wr_free[wsl] = d_mm_last
                    bc_free = [d_us_last[0], d_us_last[1]]
                    for mp in range(4):
                        wsl2 = mp % 2
                        wait(g_, wr_free[wsl2])
                        inst = g_.dma_start(
                            out=wring[wsl2][:, 0:32 * P]
                            .rearrange("p (x c) -> p x c", c=P),
                            in_=W["l0f_w2p"][e, mp]
                            .rearrange("f a p c -> p (f a) c"))
                        d_w2 = wr_sems[wsl2].inc(inst, 16)
                        d_o = None
                        for mi in range(2):
                            m = 2 * mp + mi
                            for j, tb in enumerate(tbs):
                                bi = 4 + j
                                wait(t_, pbdep[bi],
                                     d_w2 if mi + j == 0 else None,
                                     *(d_h[j] if mp == 0 and mi == 0 else []))
                                for fc in range(16):
                                    inst = t_.matmul(
                                        pb[bi][:, 0:QB],
                                        wring[wsl2][:, (mi * 16 + fc) * P:
                                                    (mi * 16 + fc + 1) * P],
                                        hsb_ap(j, fc),
                                        start=(fc == 0), stop=(fc == 15))
                                d_o = sp.inc(inst)
                                wait(v_, d_o)
                                if p0 == 0 and e == 0 and mp == 0:
                                    v_.drain()
                                inst = v_.tensor_tensor(Xc(m, tb * QB, QB),
                                                        Xc(m, tb * QB, QB),
                                                        pb[bi][:, 0:QB], ALU.add)
                                d_x = sv.inc(inst)
                                pbdep[bi] = d_x
                        wr_free[wsl2] = d_o
                    h_free = sp.now()

            v_.drain()
            spf = [None, None]
            for c in range(NCH):
                for tb in range(NQB):
                    r3 = (c * NQB + tb) % 2
                    wait(sy, spf[r3], d_spill)
                    inst = sy.dma_start(
                        out=spl_f[:, r3 * QB:(r3 + 1) * QB],
                        in_=spill[:, c * T + tb * QB:c * T + tb * QB + QB])
                    d_sp2 = spl_sems[r3].inc(inst, 16)
                    wait(v_, d_sp2)
                    inst = v_.tensor_tensor(Xc(c, tb * QB, QB), Xc(c, tb * QB, QB),
                                            spl_f[:, r3 * QB:(r3 + 1) * QB], ALU.add)
                    spf[r3] = sv.inc(inst)
            abf.reset(m0)
            af32.reset(mf)

        def l1_attn():
            block_barrier()
            pt = XPTS.index("l1a")
            m0, mf = abf.mark(), af32.mark()
            bia_ap, d_b = load_bia(["l1a_bq", "l1a_bqs", "l1a_bk", "l1a_bks"])
            bvb, d_bvb = load_f32("l1a_bvb")
            rring = af32.alloc(4 * QB)
            scr = af32.alloc(3 * QB)
            rsem = [dgrp(), dgrp()]
            rfree = [None, None]

            rring2 = [abf.alloc(16 * P) for _ in range(2)]
            rr_sems = [dgrp(sw=True), dgrp(sw=True)]
            rr_free = [None, None]

            def rope_path(tb, wA, wB, bA, bB, out8, slot_war):
                rhs = [XBFc(c, tb * QB, QB) for c in range(NCH)]
                d_last = None
                for m in range(NCH):
                    rsl = m % 2
                    wait(sy, rfree[rsl])
                    i1 = sy.dma_start(out=rring[:, rsl * 2 * QB:rsl * 2 * QB + QB],
                                      in_=W["l1a_cos"][m][:, tb * QB:(tb + 1) * QB])
                    rsem[rsl].inc(i1, 16)
                    i2 = sy.dma_start(
                        out=rring[:, rsl * 2 * QB + QB:rsl * 2 * QB + 2 * QB],
                        in_=W["l1a_sin"][m][:, tb * QB:(tb + 1) * QB])
                    d_rt = rsem[rsl].inc(i2, 16)
                    wait(g_, rr_free[rsl])
                    iw = g_.dma_start(
                        out=rring2[rsl][:, 0:8 * P].rearrange("p (a c) -> p a c", c=P),
                        in_=W[wA][m * 8:(m + 1) * 8].rearrange("a p c -> p a c"))
                    rr_sems[rsl].inc(iw, 16)
                    iw = g_.dma_start(
                        out=rring2[rsl][:, 8 * P:16 * P]
                        .rearrange("p (a c) -> p a c", c=P),
                        in_=W[wB][m * 8:(m + 1) * 8].rearrange("a p c -> p a c"))
                    d_wm = rr_sems[rsl].inc(iw, 16)
                    dmm = proj(m % 2, rring2[rsl][:, 0:8 * P], 8, 1, 0, rhs, [d_wm])
                    d_q = evac_act(m % 2, scr[:, 0:QB], AF.Copy, bias=bia_ap(bA, m),
                                   dep=dmm, war=state.get("l1r0"))
                    dmm2 = proj(2 + m % 2, rring2[rsl][:, 8 * P:16 * P], 8, 1, 0,
                                rhs, [d_wm])
                    rr_free[rsl] = dmm2
                    d_qs = evac_act(2 + m % 2, scr[:, QB:2 * QB], AF.Copy,
                                    bias=bia_ap(bB, m), dep=dmm2,
                                    war=state.get("l1r1"))
                    wait(v_, d_q, d_qs, d_rt, slot_war if m < 2 else None)
                    v_.tensor_tensor(scr[:, 2 * QB:3 * QB], scr[:, 0:QB],
                                     rring[:, rsl * 2 * QB:rsl * 2 * QB + QB],
                                     ALU.mult)
                    v_.drain()
                    state["l1r0"] = sv.inc(v_.nop()) if False else None
                    inst = v_.tensor_tensor(scr[:, 0:QB], scr[:, QB:2 * QB],
                                            rring[:, rsl * 2 * QB + QB:
                                                  rsl * 2 * QB + 2 * QB], ALU.mult)
                    d_t2 = sv.inc(inst)
                    state["l1r1"] = d_t2
                    v_.drain()
                    inst = v_.tensor_tensor(out8[:, m * QB:(m + 1) * QB],
                                            scr[:, 2 * QB:3 * QB], scr[:, 0:QB],
                                            ALU.add)
                    d_last = sv.inc(inst)
                    state["l1r0"] = d_last
                    rfree[rsl] = d_last
                return d_last

            mA = abf.mark()
            kst = abf.alloc(NCH * QB * 2)
            wait(t_, d_b)
            wait(v_, d_b)
            kx_sem = [dgrp(sw=True), dgrp(sw=True)]
            st_free = [None, None]
            xw = []
            for tb in range(NQB):
                r = tb % 2
                kst_r = kst[:, r * NCH * QB:(r + 1) * NCH * QB]
                d_kst = rope_path(tb, "l1a_wk", "l1a_wks", "l1a_bk", "l1a_bks",
                                  kst_r, st_free[r])
                dk = xwrite_k(pt, kst_r.rearrange("p (c t) -> p c t", t=QB), tb,
                              [d_kst], kx_sem[r])
                xw.append(dk)
                st_free[r] = dk
            abf.reset(mA)
            wv_w, dv1 = load_w("l1a_wv")
            vst = abf.alloc(KTB * VW * 2)
            d_vini = sv.inc(v_.memset(vst[:], 1.0))
            wait(t_, dv1)
            vx_sem = [dgrp(sw=True), dgrp(sw=True)]
            vst_free = [None, None]
            for tb in range(NQB):
                r = tb % 2
                vst_r = vst[:, r * KTB * VW:(r + 1) * KTB * VW]
                d_v = []
                for n in range(2):
                    bi = 4 + n
                    for tt in range(KTB):
                        wait(t_, pbdep[bi])
                        for kc in range(NCH):
                            inst = t_.matmul(
                                pb[bi][:, 0:512],
                                XBFc(kc, tb * QB + tt * P, P),
                                wv_w[:, (kc * 2 + n) * 512:(kc * 2 + n) * 512 + 512],
                                start=(kc == 0), stop=(kc == NCH - 1))
                        d_mmv = sp.inc(inst)
                        wait(v_, d_mmv, d_vini, vst_free[r], d_bvb)
                        inst = v_.tensor_tensor(
                            vst_r[:, tt * VW + n * 512: tt * VW + n * 512 + 512],
                            pb[bi][:, 0:512],
                            bvb[:, n * 512:(n + 1) * 512], ALU.add)
                        d_vv = sv.inc(inst)
                        pbdep[bi] = d_vv
                        d_v.append(d_vv)
                dv = xwrite_v(pt, vst_r.rearrange("p (n w) -> p n w", w=VW), tb,
                              d_v, vx_sem[r])
                xw.append(dv)
                vst_free[r] = dv
            xflag(pt, xw)
            d_ready = xpoll(pt)

            abf.reset(mA)
            qsb = abf.alloc(NCH * QB)
            krd = [abf.alloc(NCH * QB) for _ in range(2)]
            vrd = [abf.alloc(KTB * 512) for _ in range(2)]
            etb = abf.alloc(4 * QB)
            et = [etb[:, k * QB:(k + 1) * QB] for k in range(4)]
            bcev = abf.alloc(QB)
            scr2 = af32.alloc(QB)
            wait(sy, d_ready)
            d_xchg = xw[-1]
            krd_sem = [dgrp(), dgrp()]
            vrd_sem = [dgrp(), dgrp()]
            krd_free = [d_xchg, d_xchg]
            vrd_free = [d_xchg, d_xchg]
            qfree = None
            xg = None

            for i in range(NQB):
                d_q = rope_path(i, "l1a_wq", "l1a_wqs", "l1a_bq", "l1a_bqs",
                                qsb, qfree)
                den_saved = None
                for p2 in range(2):
                    et_dep = {}
                    po = (pb[2], pb[3], pb[4], pb[5])
                    inst_pv = None
                    for g2 in range(NBLK):
                        vsl = g2 % 2
                        wait(sy, krd_free[vsl])
                        i1 = sy.dma_start(
                            out=krd[vsl].rearrange("p (c t) -> p c t", t=QB),
                            in_=ksh[pt][g2 % 2].rearrange("p (c t) -> p c t", t=T)
                            [:, :, (g2 // 2) * QB:(g2 // 2 + 1) * QB])
                        d_k = krd_sem[vsl].inc(i1, 16)
                        wait(sy, vrd_free[vsl])
                        i2 = sy.dma_start(
                            out=vrd[vsl][:, 0:KTB * 512]
                            .rearrange("p (n w) -> p n w", w=512),
                            in_=vsh[pt][g2 % 2][(g2 // 2) * KTB:(g2 // 2 + 1) * KTB,
                                                :, p2 * 512:(p2 + 1) * 512]
                            .rearrange("n p w -> p n w"))
                        d_v2 = vrd_sem[vsl].inc(i2, 16)
                        wait(t_, d_k, d_v2, d_q)
                        for kt0 in range(0, KTB, 2):
                            kts = list(range(kt0, min(kt0 + 2, KTB)))
                            g0 = g2 * KTB + kt0
                            w_extra = [pbdep[0], pbdep[1]] if g0 == 0 else []
                            wait(t_, et_dep.get(g0 - 2),
                                 et_dep.get(g0 + len(kts) - 3), *w_extra)
                            d_sc = {}
                            for kt in kts:
                                gkt = g2 * KTB + kt
                                sb = gkt % 2
                                for kc in range(NCH):
                                    inst = t_.matmul(
                                        pb[sb][:, 0:QB],
                                        krd[vsl][:, kc * QB + kt * P:
                                                 kc * QB + kt * P + P],
                                        qsb[:, kc * QB:(kc + 1) * QB],
                                        start=(kc == 0), stop=(kc == NCH - 1))
                                d_sc[gkt] = sp.inc(inst)
                            wait(s_, d_sc[g0 + len(kts) - 1])
                            inst = s_.activation(
                                etb.rearrange("p (k q) -> p k q", q=QB)
                                [:, g0 % 4:g0 % 4 + len(kts), :],
                                pb01_t[:, 0:1024]
                                .rearrange("p (k q) -> p k q", q=512)
                                [:, 0:len(kts), 0:QB],
                                AF.Exp, bias=cst(-C_L1A), scale=cst(1.0 / 32.0))
                            d_e = sa.inc(inst)
                            pbdep[0] = d_e
                            pbdep[1] = d_e
                            for kt in kts:
                                et_dep[g2 * KTB + kt] = d_e
                            wait(t_, d_e)
                            for kt in kts:
                                gkt = g2 * KTB + kt
                                eslot = et[gkt % 4]
                                if gkt == 0:
                                    wait(t_, pbdep[2], pbdep[3], pbdep[4], pbdep[5],
                                         pbdep[6])
                                for dcl in range(4):
                                    inst_pv = t_.matmul(
                                        po[dcl][:, 0:QB],
                                        vrd[vsl][:, kt * 512 + dcl * P:
                                                 kt * 512 + dcl * P + P],
                                        eslot,
                                        start=(gkt == 0),
                                        stop=(gkt == NBLK * KTB - 1))
                                if p2 == 0:
                                    inst_pv = t_.matmul(
                                        pb[6][0:1, 0:QB], ONEb[:, 0:1], eslot,
                                        start=(gkt == 0),
                                        stop=(gkt == NBLK * KTB - 1))
                        d_g2 = sp.inc(inst_pv)
                        krd_free[vsl] = d_g2
                        vrd_free[vsl] = d_g2
                    d_po = krd_free[(NBLK - 1) % 2]
                    if p2 == 0:
                        wait(v_, d_po)
                        inst = v_.reciprocal(ROWS[0:1, 0:QB], pb[6][0:1, 0:QB])
                        v_.drain()
                        d_r = sv.inc(inst)
                        pbdep[6] = d_r
                        wait(t_, d_r, pbdep[0])
                        inst = t_.matmul(pb[0][:, 0:QB], ONEf[0:1, 0:P],
                                         ROWS[0:1, 0:QB], start=True, stop=True)
                        d_bcm = sp.inc(inst)
                        den_saved = evac_act(0, bcev, dep=d_bcm)
                    for dcl in range(4):
                        c = p2 * 4 + dcl
                        wait(v_, d_po, den_saved, xg)
                        inst = v_.tensor_tensor(scr2[:, 0:QB], po[dcl][:, 0:QB],
                                                bcev, ALU.mult)
                        sv.inc(inst)
                        v_.drain()
                        inst = v_.tensor_tensor(Xc(c, i * QB, QB), Xc(c, i * QB, QB),
                                                scr2[:, 0:QB], ALU.add)
                        d_x = sv.inc(inst)
                        pbdep[2 + dcl] = d_x
                qfree = sp.now()
            abf.reset(m0)
            af32.reset(mf)

        def l1_ffn():
            block_barrier()
            m0, mf = abf.mark(), af32.mark()
            oh16_sb, d_oh = load_w("ident16")
            idt_sb, d_id = load_w("identt")
            rt_w, d_rt = load_w("l1f_router")
            weT = abf.alloc(T)
            webf = abf.alloc(16)
            bcw = [abf.alloc(QB) for _ in range(2)]
            sgb = abf.alloc(QB * 2)
            hsb = abf.alloc(32 * QB)
            wring = [abf.alloc(16 * P) for _ in range(2)]
            rsc = af32.alloc(64)
            wr_sems = [dgrp(sw=True), dgrp(sw=True)]
            wr_free = [None, None]
            wait(t_, d_oh, d_id, d_rt)

            def hss_ap(j, s):
                s = j * 16 + s
                return hsb[:, s * QB:(s + 1) * QB]

            def hr_ap(j, s):
                s = j * 8 + s
                return hsb[:, s * QB:(s + 1) * QB]

            def xsc(j, c):
                if j == 0:
                    return scb_ln[:, c * QB:(c + 1) * QB]
                return hsb[:, (16 + c) * QB:(17 + c) * QB]

            lg_free = None
            d_weT = None
            for tt in range(T // P):
                bi = tt % 2
                wait(t_, pbdep[bi])
                for kc in range(NCH):
                    inst = t_.matmul(pb[bi][:, 0:16],
                                     XBFc(kc, tt * P, P),
                                     rt_w[:, kc * 16:(kc + 1) * 16],
                                     start=(kc == 0), stop=(kc == NCH - 1))
                d_lg = sp.inc(inst)
                lg = rsc[:, 0:16]
                m1 = rsc[:, 16:17]
                eq = rsc[:, 32:48]
                sg2 = rsc[:, 48:64]
                d_ev = evac_act(bi, lg, dep=d_lg, war=lg_free, n=16)
                wait(v_, d_ev)
                v_.tensor_reduce(m1, lg, mybir.AxisListType.X, ALU.max)
                v_.drain()
                inst = v_.tensor_scalar(eq, lg, m1, None, ALU.is_ge)
                v_.drain()
                d_eq = sv.inc(inst)
                wait(s_, d_ev)
                inst = s_.activation(sg2, lg, AF.Sigmoid)
                s_.drain()
                d_sg = sa.inc(inst)
                wait(v_, d_sg, d_eq)
                inst = v_.tensor_tensor(eq, eq, sg2, ALU.mult)
                v_.drain()
                inst = v_.tensor_copy(webf[:, 0:16], eq)
                v_.drain()
                d_web = sv.inc(inst)
                lg_free = d_web
                wait(t_, d_web, pbdep[7])
                inst = t_.transpose(pbT[0:16, 0:P], webf[:, 0:16], idt_sb)
                d_tr = sp.inc(inst)
                wait(s_, d_tr)
                inst = s_.activation(weT[0:16, tt * P:(tt + 1) * P],
                                     pbT[0:16, 0:P], AF.Copy)
                d_weT = sa.inc(inst)
                pbdep[7] = d_weT

            sg_free = [None, None]
            h_free = None
            for p0 in range(0, NQB, 2):
                tbs = tuple(range(p0, min(p0 + 2, NQB)))
                rhs2 = [[XBFc(c, tb * QB, QB) for c in range(NCH)] for tb in tbs]
                for half in range(2):
                    d_h = [[] for _ in tbs]
                    for fci in range(16):
                        fc = half * 16 + fci
                        wsl = fci % 2
                        wait(g_, wr_free[wsl])
                        inst = g_.dma_start(
                            out=wring[wsl][:, 0:16 * P]
                            .rearrange("p (x c) -> p x c", c=P),
                            in_=W["l1f_sgu"][fc].rearrange("s a p c -> p (s a) c"))
                        d_w = wr_sems[wsl].inc(inst, 16)
                        d_mm_last = None
                        for j, tb in enumerate(tbs):
                            bi = j
                            bj = 2 + j
                            dmm = proj(bi, wring[wsl][:, 0:8 * P], 8, 1, 0,
                                       rhs2[j], [d_w] if j == 0 else [])
                            d_sg = evac_act(bi, sgb[:, j * QB:(j + 1) * QB],
                                            AF.Silu, dep=dmm, war=sg_free[j])
                            dmm2 = proj(bj, wring[wsl][:, 8 * P:16 * P], 8, 1, 0,
                                        rhs2[j], [])
                            d_mm_last = dmm2
                            wait(v_, dmm2, d_sg, h_free)
                            inst = v_.tensor_tensor(hss_ap(j, fci),
                                                    sgb[:, j * QB:(j + 1) * QB],
                                                    pb[bj][:, 0:QB], ALU.mult)
                            dh = sv.inc(inst)
                            pbdep[bj] = dh
                            d_h[j].append(dh)
                            sg_free[j] = dh
                        wr_free[wsl] = d_mm_last
                    for m in range(NCH):
                        wsl2 = m % 2
                        wait(g_, wr_free[wsl2])
                        inst = g_.dma_start(
                            out=wring[wsl2][:, 0:16 * P]
                            .rearrange("p (x c) -> p x c", c=P),
                            in_=W["l1f_sd"][half * 16:(half + 1) * 16, m]
                            .rearrange("a p c -> p a c"))
                        d_w2 = wr_sems[wsl2].inc(inst, 16)
                        d_o = None
                        for j, tb in enumerate(tbs):
                            bi = 4 + j
                            wait(t_, pbdep[bi], d_w2 if j == 0 else None,
                                 *(d_h[j] if m == 0 else []))
                            for fci in range(16):
                                inst = t_.matmul(
                                    pb[bi][:, 0:QB],
                                    wring[wsl2][:, fci * P:(fci + 1) * P],
                                    hss_ap(j, fci),
                                    start=(fci == 0), stop=(fci == 15))
                            d_o = sp.inc(inst)
                            wait(v_, d_o)
                            inst = v_.tensor_tensor(Xc(m, tb * QB, QB),
                                                    Xc(m, tb * QB, QB),
                                                    pb[bi][:, 0:QB], ALU.add)
                            pbdep[bi] = sv.inc(inst)
                        wr_free[wsl2] = d_o
                    h_free = sp.now()

            bc_free = [None, None]
            xs_free = [h_free, h_free]
            for p0 in range(0, NQB, 2):
                tbs = tuple(range(p0, min(p0 + 2, NQB)))
                for e in range(16):
                    d_xs = [[] for _ in tbs]
                    for j, tb in enumerate(tbs):
                        wait(t_, pbdep[6], d_weT, bc_free[j])
                        inst = t_.matmul(pb[6][:, 0:QB],
                                         oh16_sb[0:16, e * P:(e + 1) * P],
                                         weT[0:16, tb * QB:(tb + 1) * QB],
                                         start=True, stop=True)
                        d_bc = sp.inc(inst)
                        d_bce = evac_act(6, bcw[j], dep=d_bc, war=bc_free[j])
                        for c in range(NCH):
                            wait(v_, d_bce if c == 0 else None,
                                 xs_free[j] if c == 0 else None)
                            inst = v_.tensor_tensor(xsc(j, c),
                                                    XBFc(c, tb * QB, QB), bcw[j],
                                                    ALU.mult)
                            d_xs[j].append(sv.inc(inst))
                        bc_free[j] = d_xs[j][-1]
                    rhs2 = [[xsc(j, c) for c in range(NCH)]
                            for j in range(len(tbs))]
                    d_h = [[] for _ in tbs]
                    for fc in range(8):
                        wsl = fc % 2
                        wait(g_, wr_free[wsl])
                        inst = g_.dma_start(
                            out=wring[wsl][:, 0:16 * P]
                            .rearrange("p (x c) -> p x c", c=P),
                            in_=W["l1f_gu"][e, fc].rearrange("s a p c -> p (s a) c"))
                        d_w = wr_sems[wsl].inc(inst, 16)
                        d_mm_last = None
                        for j, tb in enumerate(tbs):
                            bi = j
                            bj = 2 + j
                            dmm = proj(bi, wring[wsl][:, 0:8 * P], 8, 1, 0,
                                       rhs2[j],
                                       ([d_w] if j == 0 else []) +
                                       (d_xs[j] if fc == 0 else []))
                            d_sg = evac_act(bi, sgb[:, j * QB:(j + 1) * QB],
                                            AF.Silu, dep=dmm, war=sg_free[j])
                            dmm2 = proj(bj, wring[wsl][:, 8 * P:16 * P], 8, 1, 0,
                                        rhs2[j], [])
                            d_mm_last = dmm2
                            wait(v_, dmm2, d_sg, h_free)
                            inst = v_.tensor_tensor(hr_ap(j, fc),
                                                    sgb[:, j * QB:(j + 1) * QB],
                                                    pb[bj][:, 0:QB], ALU.mult)
                            dh = sv.inc(inst)
                            pbdep[bj] = dh
                            d_h[j].append(dh)
                            sg_free[j] = dh
                        wr_free[wsl] = d_mm_last
                    xs_free = [sp.now(), sp.now()]
                    for mp in range(4):
                        wsl2 = mp % 2
                        wait(g_, wr_free[wsl2])
                        inst = g_.dma_start(
                            out=wring[wsl2][:, 0:16 * P]
                            .rearrange("p (x c) -> p x c", c=P),
                            in_=W["l1f_wdp"][e, mp]
                            .rearrange("f a p c -> p (f a) c"))
                        d_w2 = wr_sems[wsl2].inc(inst, 16)
                        d_o = None
                        for mi in range(2):
                            m = 2 * mp + mi
                            for j, tb in enumerate(tbs):
                                bi = 4 + j
                                wait(t_, pbdep[bi],
                                     d_w2 if mi + j == 0 else None,
                                     *(d_h[j] if mp == 0 and mi == 0 else []))
                                for fc in range(8):
                                    inst = t_.matmul(
                                        pb[bi][:, 0:QB],
                                        wring[wsl2][:, (mi * 8 + fc) * P:
                                                    (mi * 8 + fc + 1) * P],
                                        hr_ap(j, fc),
                                        start=(fc == 0), stop=(fc == 7))
                                d_o = sp.inc(inst)
                                wait(v_, d_o)
                                inst = v_.tensor_tensor(Xc(m, tb * QB, QB),
                                                        Xc(m, tb * QB, QB),
                                                        pb[bi][:, 0:QB], ALU.add)
                                pbdep[bi] = sv.inc(inst)
                        wr_free[wsl2] = d_o
                    h_free = sp.now()
            abf.reset(m0)
            af32.reset(mf)

        scb_ln = abf.alloc(NCH * QB)
        scrl_ln = af32.alloc(QB)
        base_abf, base_f32 = abf.mark(), af32.mark()

        if "l0a" in blocks:
            l0_attn()
            if dbg != 2:
                postln(scb=scb_ln, scrl=scrl_ln)
        if "l0f" in blocks:
            l0_ffn()
            postln(scb=scb_ln, scrl=scrl_ln)
        if "l1a" in blocks:
            l1_attn()
            postln(scb=scb_ln, scrl=scrl_ln)
        if "l1f" in blocks:
            l1_ffn()
            postln(scb=scb_ln, scrl=scrl_ln)

        d_fin = sv.now()
        wait(sy, d_fin, sa.now())
        d_y = dgrp().inc(sy.dma_start(out=y_out.rearrange("(c p) t -> p c t", p=P),
                                      in_=X[:].rearrange("p (c t) -> p c t", t=T)), 16)
        wait(sy, d_y)

    with nc.Block() as block:
        @block.sync
        def _(sy):
            emit(sy)

    stk.close()
    return nc



def _prep_l0a(iv, T, parity, rows):
    out = {}
    out["l0a_wq"] = _tw(iv["L0_attn_wq"]); out["l0a_bq"] = _bpm(iv["L0_attn_bq"])
    out["l0a_wqc"] = _tw(iv["L0_attn_wqc"]); out["l0a_bqc"] = _bpm(iv["L0_attn_bqc"])
    perm = np.concatenate([np.arange(64, 128), np.arange(0, 64)])
    out["l0a_wqr"] = _tw(iv["L0_attn_wqr"]); out["l0a_bqr"] = _bpm(iv["L0_attn_bqr"])
    out["l0a_wqrs"] = _tw(iv["L0_attn_wqr"][:, perm])
    out["l0a_bqrs"] = _bpm(iv["L0_attn_bqr"][perm])
    out["l0a_wkv"] = _tw(iv["L0_attn_wkv"]); out["l0a_bkv"] = _bpm(iv["L0_attn_bkv"])
    out["l0a_wkr"] = _tw(iv["L0_attn_wkr"]); out["l0a_bkr"] = _bpm(iv["L0_attn_bkr"])
    out["l0a_wkrs"] = _tw(iv["L0_attn_wkr"][:, perm])
    out["l0a_bkrs"] = _bpm(iv["L0_attn_bkr"][perm])
    out["l0a_wkc"] = _tw(iv["L0_attn_wkc"]); out["l0a_bkc"] = _bpm(iv["L0_attn_bkc"])
    out["l0a_wv"] = _trhs(iv["L0_attn_wv"])
    out["l0a_bvb"] = np.ascontiguousarray(
        np.broadcast_to(iv["L0_attn_bv"][None, :], (P, D)))
    out["l0a_wo"] = _who(iv["L0_attn_wo"]); out["l0a_bo"] = _bpm(iv["L0_attn_bo"])
    pos = rows.astype(np.float64)
    inv = 1.0 / (10000.0 ** (np.arange(0, 128, 2, dtype=np.float64) / 128.0))
    fr = (pos[None, :] * inv[:, None])
    cos = np.cos(fr).astype(np.float32)
    sin = np.sin(fr).astype(np.float32)
    out["l0a_cos"] = np.ascontiguousarray(np.concatenate([cos, cos], 0))
    out["l0a_sin"] = np.ascontiguousarray(np.concatenate([-sin, sin], 0))
    QBl = min(512, T)
    KTBl = QBl // P
    m = np.zeros((2 * KTBl, P, QBl), np.float32)
    for j in range(2 * KTBl):
        gk = P * j + np.arange(P)[:, None]
        gq = QBl * parity + np.arange(QBl)[None, :]
        m[j] = (gk <= gq).astype(np.float32)
    out["l0a_mask"] = m
    return out


def _prep_l0f(iv):
    out = {}
    out["l0f_g1"] = _bpm(iv["L0_ffn_ln1_w"])
    out["l0f_g2"] = _bpm(iv["L0_ffn_ln2_w"])
    out["l0f_wq"] = _twm(np.ascontiguousarray(iv["L0_ffn_wq"].T))
    out["l0f_wk"] = _twm(np.ascontiguousarray(iv["L0_ffn_wk"].T))
    out["l0f_wv"] = _trhs(np.ascontiguousarray(iv["L0_ffn_wv"].T))
    out["l0f_wo"] = _who(np.ascontiguousarray(iv["L0_ffn_wo"].T))
    out["l0f_gate"] = _trhs(np.ascontiguousarray(iv["L0_ffn_gate_w"].T), n=8)
    w1 = np.stack([_tw(np.ascontiguousarray(iv["L0_ffn_w1"][e].T))
                   .reshape(8, 16, P, P) for e in range(8)])
    w3 = np.stack([_tw(np.ascontiguousarray(iv["L0_ffn_w3"][e].T))
                   .reshape(8, 16, P, P) for e in range(8)])
    out["l0f_w13"] = np.ascontiguousarray(
        np.stack([np.stack([np.stack([w1[e][:, 2 * fp], w3[e][:, 2 * fp],
                                      w1[e][:, 2 * fp + 1], w3[e][:, 2 * fp + 1]],
                                     axis=0)
                            for fp in range(8)]) for e in range(8)]))
    w2 = np.stack([_tw(np.ascontiguousarray(iv["L0_ffn_w2"][e].T))
                   .reshape(16, 8, P, P) for e in range(8)])
    out["l0f_w2p"] = np.ascontiguousarray(
        np.stack([np.stack([np.stack([w2[e][:, 2 * mp], w2[e][:, 2 * mp + 1]],
                                     axis=0)
                            for mp in range(4)]) for e in range(8)]))
    return out


def _prep_const():
    out = {}
    oh8 = np.zeros((8, 8, P), np.float32)
    for e in range(8):
        oh8[e, e, :] = 1.0
    out["ident8"] = oh8
    oh16 = np.zeros((16, 16, P), np.float32)
    for e in range(16):
        oh16[e, e, :] = 1.0
    out["ident16"] = oh16
    out["identt"] = np.eye(P, dtype=np.float32)
    return out


def _prep_l1a(iv, T, rows):
    out = {}
    perm = np.arange(D).reshape(-1, 2)[:, ::-1].reshape(-1)
    wq = np.ascontiguousarray(iv["L1_attn_wq_w"].T)
    wk = np.ascontiguousarray(iv["L1_attn_wk_w"].T)
    out["l1a_wq"] = _twm(wq); out["l1a_bq"] = _bpm(iv["L1_attn_wq_b"])
    out["l1a_wqs"] = _twm(np.ascontiguousarray(wq[:, perm]))
    out["l1a_bqs"] = _bpm(iv["L1_attn_wq_b"][perm])
    out["l1a_wk"] = _twm(wk); out["l1a_bk"] = _bpm(iv["L1_attn_wk_b"])
    out["l1a_wks"] = _twm(np.ascontiguousarray(wk[:, perm]))
    out["l1a_bks"] = _bpm(iv["L1_attn_wk_b"][perm])
    out["l1a_wv"] = _trhs(np.ascontiguousarray(iv["L1_attn_wv_w"].T))
    out["l1a_bvb"] = np.ascontiguousarray(
        np.broadcast_to(iv["L1_attn_wv_b"][None, :], (P, D)))
    inv = 1.0 / (10000.0 ** (np.arange(0, D, 2, dtype=np.float64) / D))
    ang = rows.astype(np.float64)[None, :] * inv[:, None]
    cosI = np.empty((D, len(rows)), np.float32)
    sinI = np.empty((D, len(rows)), np.float32)
    cosI[0::2] = np.cos(ang); cosI[1::2] = np.cos(ang)
    sinI[0::2] = -np.sin(ang); sinI[1::2] = np.sin(ang)
    out["l1a_cos"] = np.ascontiguousarray(cosI.reshape(8, P, len(rows)))
    out["l1a_sin"] = np.ascontiguousarray(sinI.reshape(8, P, len(rows)))
    return out


def _prep_l1f(iv):
    out = {}
    out["l1f_router"] = _trhs(np.ascontiguousarray(iv["L1_ffn_router_w"].T), n=16)
    wgu = np.stack([_tw(np.ascontiguousarray(iv["L1_ffn_gate_up_proj"][e]))
                    .reshape(8, 16, P, P) for e in range(16)])
    out["l1f_gu"] = np.ascontiguousarray(
        np.stack([np.stack([np.stack([wgu[e][:, fc], wgu[e][:, fc + 8]], axis=0)
                            for fc in range(8)]) for e in range(16)]))
    wd = np.stack([_tw(np.ascontiguousarray(iv["L1_ffn_down_proj"][e]))
                   .reshape(8, 8, P, P) for e in range(16)])
    out["l1f_wdp"] = np.ascontiguousarray(
        np.stack([np.stack([np.stack([wd[e][:, 2 * mp], wd[e][:, 2 * mp + 1]],
                                     axis=0)
                            for mp in range(4)]) for e in range(16)]))
    sg = _tw(np.ascontiguousarray(iv["L1_ffn_shared_gate_w"].T)).reshape(8, 32, P, P)
    su = _tw(np.ascontiguousarray(iv["L1_ffn_shared_up_w"].T)).reshape(8, 32, P, P)
    out["l1f_sgu"] = np.ascontiguousarray(
        np.stack([np.stack([sg[:, fc], su[:, fc]], axis=0) for fc in range(32)]))
    out["l1f_sd"] = _tw(np.ascontiguousarray(iv["L1_ffn_shared_down_w"].T)).reshape(32, 8, P, P)
    return out


def make_inputs(iv, T=2048, blocks=("l0a", "l0f", "l1a", "l1f")):
    x = np.asarray(iv["L0_attn_x"])
    nn = np.array([[np.random.randint(1, 2 ** 30)]], dtype=np.int32)
    shared = {"nonce": nn}
    if "l0f" in blocks:
        shared.update(_prep_l0f(iv))
    if "l1f" in blocks:
        shared.update(_prep_l1f(iv))
    shared.update(_prep_const())
    par = [dict(shared) for _ in range(2)]
    for parity in range(2):
        rows = own_rows(parity, T)
        if "l0a" in blocks:
            par[parity].update(_prep_l0a(iv, T, parity, rows))
        if "l1a" in blocks:
            par[parity].update(_prep_l1a(iv, T, rows))
    maps = []
    for core in range(8):
        b = core // 2
        parity = core % 2
        rows = own_rows(parity, T)
        im = dict(par[parity])
        im["x_fm"] = np.ascontiguousarray(np.asarray(x[b])[rows].T)
        maps.append(im)
    return maps


def assemble(results, T=2048, B=4, S=4096):
    out = np.zeros((B, S, D), np.float32)
    for core in range(8):
        b = core // 2
        rows = own_rows(core % 2, T)
        out[b][rows] = results[core]["y_fm"].T
    return out


def kernel(**inputs):
    T = 2048
    blocks = ("l0a", "l0f", "l1a", "l1f")
    maps = make_inputs(inputs, T=T, blocks=blocks)
    nc = build_program(T=T, blocks=blocks)
    res = run_bass_kernel_spmd(nc, maps, core_ids=list(range(8)))
    return assemble(res.results, T=T)


# version 06
# speedup vs baseline: 1.0502x; 1.0502x over previous
import numpy as np
import contextlib
import concourse.bass as bass
import concourse.mybir as mybir
from concourse.bass_utils import run_bass_kernel_spmd

F32 = mybir.dt.float32
BF16 = mybir.dt.bfloat16
I32 = mybir.dt.int32
AF = mybir.ActivationFunctionType
ALU = mybir.AluOpType

P = 128
D = 1024
NCH = 8
NH = 16
HD = 64
VW = NH * (HD + 1)
C_L0A = 4.0
C_L0F = 12.0
C_L1A = 12.0
EPS = 1e-5



def _tw(w):
    din, dout = w.shape
    return np.ascontiguousarray(w.reshape(din // P, P, dout // P, P).transpose(0, 2, 1, 3)).reshape(-1, P, P)


def _trhs(w, n=512):
    din, dout = w.shape
    n = min(n, dout)
    return np.ascontiguousarray(
        w.reshape(din // P, P, dout // n, n).transpose(0, 2, 1, 3)).reshape(-1, P, n)


def _twm(w):
    din, dout = w.shape
    return np.ascontiguousarray(
        w.reshape(din // P, P, dout // P, P).transpose(2, 0, 1, 3)).reshape(-1, P, P)


def _bpm(b):
    return np.ascontiguousarray(b.reshape(-1, P).T)


def _who(w):
    return np.ascontiguousarray(
        w.reshape(NH, HD, NCH, P).transpose(2, 0, 1, 3)).reshape(-1, HD, P)


def own_rows(parity, T):
    B = min(512, T)
    return np.concatenate([np.arange((2 * i + parity) * B, (2 * i + parity + 1) * B)
                           for i in range(T // B)])



class Dep:
    __slots__ = ("sem", "val")

    def __init__(self, sem, val):
        self.sem = sem
        self.val = val


class Sem:
    def __init__(self, h):
        self.h = h
        self.n = 0

    def inc(self, inst, k=1):
        inst.then_inc(self.h, k)
        self.n += k
        return Dep(self, self.n)

    def now(self):
        return Dep(self, self.n)


def wait(eng, *deps):
    best = {}
    for d in deps:
        if d is None:
            continue
        k = id(d.sem)
        if k not in best or best[k].val < d.val:
            best[k] = d
    for d in best.values():
        if d.val > 0:
            eng.wait_ge(d.sem.h, d.val)


class Arena:
    def __init__(self, buf, cols):
        self.buf = buf
        self.cols = cols
        self.off = 0

    def alloc(self, n):
        import traceback
        if self.off + n > self.cols:
            frames = " | ".join(f"{t.name}:{t.lineno}" for t in traceback.extract_stack()[-5:-1])
            raise AssertionError(
                f"arena overflow {self.off}+{n}>{self.cols} at {frames}")
        o = self.off
        self.off += n
        return self.buf[:, o:o + n]

    def mark(self):
        return self.off

    def reset(self, m):
        self.off = m



def build_program(T=2048, blocks=("l0a", "l0f", "l1a", "l1f"), race_detect=True, dbg=0):
    nc = bass.Bass("TRN2", target_bir_lowering=False,
                   detect_race_conditions=race_detect)
    QB = min(512, T)
    NQB = T // QB
    KTB = QB // P
    NBLK = (2 * T) // QB
    NVT = T // P

    stk = contextlib.ExitStack()

    def dram(name, shape, dt, kind="ExternalInput", addr_space="Local"):
        return nc.dram_tensor(name, shape, dt, kind=kind, addr_space=addr_space).ap()

    def sbuf(name, shape, dt):
        return stk.enter_context(nc.sbuf_tensor(name, shape, dt))

    def mksem(name):
        return Sem(stk.enter_context(nc.semaphore(name)))

    x_in = dram("x_fm", [D, T], F32)
    y_out = dram("y_fm", [D, T], F32, kind="ExternalOutput")
    QBv = min(512, T)
    dbg_out = dram("dbg", [3, P, max(NCH * QBv, (QBv // P) * VW)], F32, kind="ExternalOutput") if dbg else None
    dbg2_out = dram("dbg2", [P, 16 * QBv], F32, kind="ExternalOutput") if dbg else None
    dbg3_out = dram("dbg3", [P, 6 * QBv + 2048], F32 if dbg in (4, 7) else BF16, kind="ExternalOutput") if dbg else None
    nonce_in = dram("nonce", [1, 1], I32)

    W = {}

    def win(name, shape):
        W[name] = dram(name, shape, F32)

    if "l0a" in blocks:
        win("l0a_wq", [16, P, P]); win("l0a_bq", [P, 2])
        win("l0a_wqc", [14, P, P]); win("l0a_bqc", [P, 7])
        win("l0a_wqr", [8, P, P]); win("l0a_bqr", [P, 1])
        win("l0a_wqrs", [8, P, P]); win("l0a_bqrs", [P, 1])
        win("l0a_wkv", [16, P, P]); win("l0a_bkv", [P, 2])
        win("l0a_wkr", [8, P, P]); win("l0a_bkr", [P, 1])
        win("l0a_wkrs", [8, P, P]); win("l0a_bkrs", [P, 1])
        win("l0a_wkc", [14, P, P]); win("l0a_bkc", [P, 7])
        win("l0a_wv", [4, P, 512]); win("l0a_bvb", [P, D])
        win("l0a_wo", [NH * NCH, HD, P]); win("l0a_bo", [P, 8])
        win("l0a_cos", [P, T]); win("l0a_sin", [P, T])
        win("l0a_mask", [2 * KTB, P, QB])

    if "l1a" in blocks:
        win("l1a_wq", [64, P, P]); win("l1a_bq", [P, 8])
        win("l1a_wqs", [64, P, P]); win("l1a_bqs", [P, 8])
        win("l1a_wk", [64, P, P]); win("l1a_bk", [P, 8])
        win("l1a_wks", [64, P, P]); win("l1a_bks", [P, 8])
        win("l1a_wv", [16, P, 512]); win("l1a_bvb", [P, D])
        win("l1a_cos", [8, P, T]); win("l1a_sin", [8, P, T])
    if "l1f" in blocks:
        win("l1f_router", [8, P, 16])
        win("l1f_gu", [16, 8, 2, 8, P, P]); win("l1f_wdp", [16, 4, 2, 8, P, P])
        win("l1f_sgu", [32, 2, 8, P, P])
        win("l1f_sd", [32, 8, P, P])
    if "l0f" in blocks:
        win("l0f_g1", [P, 8]); win("l0f_g2", [P, 8])
        win("l0f_wq", [64, P, P]); win("l0f_wk", [64, P, P])
        win("l0f_wv", [16, P, 512]); win("l0f_wo", [NH * NCH, HD, P])
        win("l0f_gate", [8, P, 8])
        win("l0f_w13", [8, 8, 4, 8, P, P])
        win("l0f_w2p", [8, 4, 2, 16, P, P])

    win("ident8", [8, 8, P])
    win("identt", [P, P])
    win("ident16", [16, 16, P])
    XPTS = [b for b in ("l0a", "l0f", "l1a") if b in blocks]
    ksh = [dram(f"ksh{i}", [2, P, NCH * T], BF16, kind="Internal", addr_space="Shared")
           for i in range(len(XPTS))]
    vsh = [dram(f"vsh{i}", [2, NVT, P, VW], BF16, kind="Internal", addr_space="Shared")
           for i in range(len(XPTS))]
    flg = dram("flg", [1, 2 * max(len(XPTS), 1)], I32, kind="Internal",
               addr_space="Shared")
    spill = dram("spillx", [P, NCH * T], F32, kind="Internal")

    X = sbuf("X", [P, NCH * T], F32)
    XBF = sbuf("XBF", [P, NCH * T], BF16)
    ABF_COLS = 32768
    AF32_COLS = 9216
    abf = Arena(sbuf("ABF", [P, ABF_COLS], BF16), ABF_COLS)
    af32 = Arena(sbuf("AF32", [P, AF32_COLS], F32), AF32_COLS)
    BIA = sbuf("BIA", [P, 64], F32)
    ROWS = sbuf("ROWSS", [P, 5 * 512], F32)
    ONEb = sbuf("ONEb", [P, P], BF16)
    ONEf = sbuf("ONEf", [P, P], F32)
    CST = sbuf("CST", [P, 16], F32)
    NON = sbuf("NON", [1, 1], I32)
    PFL = sbuf("PFL", [1, 1], I32)
    JNK = sbuf("JNK", [P, P], BF16)

    pb01_t = stk.enter_context(nc.psum_tensor("pb01", [P, 1024], F32))
    pb2345_t = stk.enter_context(nc.psum_tensor("pb2345", [P, 2048], F32))
    pb6_t = stk.enter_context(nc.psum_tensor("pb6", [P, 512], F32))
    pbT = stk.enter_context(nc.psum_tensor("pbT", [P, 1024], BF16))
    pb = ([pb01_t[:, 0:512], pb01_t[:, 512:1024]]
          + [pb2345_t[:, k * 512:(k + 1) * 512] for k in range(4)]
          + [pb6_t[:, 0:512]])
    pbdep = [None] * 8

    sp = mksem("sp")
    sa = mksem("sa")
    sv = mksem("sv")
    spoll = mksem("spoll")
    sready = mksem("sready")
    dpool_hw = [mksem(f"dph{i}") for i in range(16)]
    dpool_sw = [mksem(f"dps{i}") for i in range(20)]
    dpi = [0, 0]

    def dgrp(sw=False):
        if sw:
            s_ = dpool_sw[dpi[1] % len(dpool_sw)]
            dpi[1] += 1
        else:
            s_ = dpool_hw[dpi[0] % len(dpool_hw)]
            dpi[0] += 1
        return s_

    def emit(sy):
        t_ = nc.tensor
        v_ = nc.vector
        s_ = nc.scalar
        g_ = nc.gpsimd

        def Xc(c, t0=0, n=None):
            n = T if n is None else n
            return X[:, c * T + t0: c * T + t0 + n]

        def XBFc(c, t0=0, n=None):
            n = T if n is None else n
            return XBF[:, c * T + t0: c * T + t0 + n]

        pid = g_.partition_id()
        r_par = g_.alloc_register("par")
        r_cond = g_.alloc_register("cond")
        r_fv = g_.alloc_register("fv")
        r_tgt = g_.alloc_register("tgt")
        g_.reg_alu(r_par, pid, 1, ALU.bitwise_and)
        g_.reg_mov(r_tgt, 0)

        d_jnk = sv.inc(v_.memset(JNK[:], 0.5))
        v_.memset(ONEb[:], 1.0)
        d_one = sv.inc(v_.memset(ONEf[:], 1.0))
        cst_map = {}

        def cst(v):
            v = float(v)
            assert v in cst_map, f"constant {v} not preregistered"
            return CST[:, cst_map[v]:cst_map[v] + 1]

        for _cv in (-C_L0A, -C_L0F, -C_L1A, EPS, 1.0 / D, 0.125, 1.0 / 32.0, -1.0,
                    -1e9, 1.0):
            _cv = float(_cv)
            if _cv not in cst_map:
                cst_map[_cv] = len(cst_map)
                v_.memset(CST[:, cst_map[_cv]:cst_map[_cv] + 1], _cv)
        d_konst = sv.inc(v_.nop()) if False else sv.inc(
            v_.memset(CST[:, 15:16], 0.0))

        d_non = dgrp().inc(sy.dma_start(out=NON[:], in_=nonce_in), 16)
        d_x = dgrp().inc(sy.dma_start(out=X[:].rearrange("p (c t) -> p c t", t=T),
                                      in_=x_in.rearrange("(c p) t -> p c t", p=P)), 16)

        wait(t_, d_jnk)
        for _ in range(20):
            t_.matmul(pb[6][:, 0:P], JNK[:], JNK[:], start=True, stop=True)
        pbdep[6] = None

        wait(s_, d_x)
        d_xbf = sa.inc(s_.activation(XBF[:], X[:], AF.Copy))
        st = {"xbf": d_xbf}
        state = st

        def load_w(name, arena=abf):
            w = W[name]
            shp = list(w.tensor.shape)
            pr, cols = shp[-2], shp[-1]
            a = int(np.prod(shp[:-2])) if len(shp) > 2 else 1
            dst = arena.alloc(a * cols)
            if len(shp) == 2:
                src = w
                dstv = dst[0:pr, :]
            else:
                assert len(shp) == 3
                src = w.rearrange("a p c -> p a c")
                dstv = dst[0:pr, :].rearrange("p (a c) -> p a c", c=cols)
            gs = dgrp(sw=True)
            return dst, gs.inc(g_.dma_start(out=dstv, in_=src), 16)

        def load_f32(name, arena=af32):
            w = W[name]
            shp = list(w.tensor.shape)
            cols = shp[-1]
            a = int(np.prod(shp[:-2])) if len(shp) > 2 else 1
            dst = arena.alloc(a * cols)
            if len(shp) == 2:
                src = w
                dstv = dst
            else:
                assert len(shp) == 3
                src = w.rearrange("a p c -> p a c")
                dstv = dst.rearrange("p (a c) -> p a c", c=cols)
            gs = dgrp()
            return dst, gs.inc(sy.dma_start(out=dstv, in_=src), 16)

        def load_bia(names):
            cols = {}
            c0 = 0
            gs = dgrp()
            last = None
            for nm in names:
                n = W[nm].tensor.shape[-1]
                last = gs.inc(sy.dma_start(out=BIA[:, c0:c0 + n], in_=W[nm]), 16)
                cols[nm] = c0
                c0 += n
            assert c0 <= 64
            return (lambda nm, m=0: BIA[:, cols[nm] + m:cols[nm] + m + 1]), last

        def wt(dst, k, m, nm, w=P, pr=P):
            return dst[0:pr, (k * nm + m) * w:(k * nm + m) * w + w]

        def proj(bi, woff, nk, nm, m, rhs, deps, n=QB, pr=P):
            wait(t_, pbdep[bi], *deps)
            for k in range(nk):
                inst = t_.matmul(pb[bi][:, 0:n], wt(woff, k, m, nm, pr=pr), rhs[k],
                                 start=(k == 0), stop=(k == nk - 1))
            return sp.inc(inst)

        def evac_act(bi, out_ap, func=AF.Copy, bias=0.0, scale=1.0, dep=None,
                     war=None, rows=None, n=None):
            n = QB if n is None else n
            wait(s_, dep, war)
            src = pb[bi][:, 0:n] if rows is None else pb[bi][rows[0]:rows[1], 0:n]
            if func == AF.Copy and not isinstance(bias, float):
                func = AF.Identity
            inst = s_.activation(out_ap, src, func, bias=bias, scale=scale)
            d = sa.inc(inst)
            pbdep[bi] = d
            return d

        def block_barrier():
            for eng in (t_, v_, s_, g_, sy):
                wait(eng, Dep(sp, sp.n), Dep(sa, sa.n), Dep(sv, sv.n))

        def pair_dma(dst0, dst1, src, sem):
            with g_.If_eq(r_par, 0):
                inst = g_.dma_start(out=dst0, in_=src)
                inst.then_inc(sem.h, 16)
            with g_.Else():
                inst = g_.dma_start(out=dst1, in_=src)
                inst.then_inc(sem.h, 16)
            sem.n += 16
            return Dep(sem, sem.n)

        def xwrite_k(pt, src, tb, deps, sem):
            wait(g_, *deps)
            kv = ksh[pt]
            return pair_dma(
                kv[0].rearrange("p (c t) -> p c t", t=T)[:, :, tb * QB:(tb + 1) * QB],
                kv[1].rearrange("p (c t) -> p c t", t=T)[:, :, tb * QB:(tb + 1) * QB],
                src, sem)

        def xwrite_v(pt, src, tb, deps, sem):
            wait(g_, *deps)
            return pair_dma(
                vsh[pt][0][tb * KTB:(tb + 1) * KTB].rearrange("n p w -> p n w"),
                vsh[pt][1][tb * KTB:(tb + 1) * KTB].rearrange("n p w -> p n w"),
                src, sem)

        def xflag(pt, deps):
            wait(g_, *deps, d_non)
            return pair_dma(flg[0:1, 2 * pt:2 * pt + 1],
                            flg[0:1, 2 * pt + 1:2 * pt + 2], NON[:], dgrp(sw=True))

        def xpoll(pt):
            nv = g_.value_load(NON[0:1, 0:1])
            g_.reg_mov(r_cond, 1)
            with g_.While(r_cond):
                with g_.If_eq(r_par, 0):
                    g_.dma_start(out=PFL[:], in_=flg[0:1, 2 * pt + 1:2 * pt + 2]) \
                        .then_inc(spoll.h, 16)
                with g_.Else():
                    g_.dma_start(out=PFL[:], in_=flg[0:1, 2 * pt:2 * pt + 1]) \
                        .then_inc(spoll.h, 16)
                g_.reg_add(r_tgt, r_tgt, 16)
                g_.wait_ge(spoll.h, r_tgt)
                g_.reg_load(r_fv, PFL[0:1, 0:1])
                g_.reg_alu(r_cond, r_fv, nv, ALU.not_equal)
            sready.n += 1
            g_.sem_inc(sready.h, 1)
            return Dep(sready, sready.n)


        def attention_mh(pt, causal, cshift, wo_load, bo_ap, qs_make, masks,
                         krd, vrd, et, etb, bcev, hva, scr2, qstate,
                         d_start=None, x_guard=None):
            krd_dep = [None, None]
            krd_free = [d_start, d_start]
            krd_sem = [dgrp(), dgrp()]
            scr_free = [None, None]
            wo_use = [None, None]
            wo_sem = [dgrp(), dgrp()]
            hva_free = None

            def stream(g2, sl, ch, h0):
                wait(sy, krd_free[sl])
                i1 = sy.dma_start(
                    out=krd[sl],
                    in_=ksh[pt][g2 % 2][:, ch * T + (g2 // 2) * QB:
                                        ch * T + (g2 // 2) * QB + QB])
                krd_sem[sl].inc(i1, 16)
                i2 = sy.dma_start(
                    out=vrd[sl].rearrange("p (n w) -> p n w", w=130),
                    in_=vsh[pt][g2 % 2][(g2 // 2) * KTB:(g2 // 2 + 1) * KTB,
                                        :, 65 * h0:65 * h0 + 130]
                    .rearrange("n p w -> p n w"))
                krd_dep[sl] = krd_sem[sl].inc(i2, 16)

            for i in range(NQB):
                ngrp = 2 * i + 2 if causal else NBLK
                nkt = ngrp * KTB
                qs, d_q = qs_make(i)
                hv_dep = [None] * NH
                prev_hv = None
                for sweep in range(NH // 2):
                    ch = sweep
                    heads = (2 * sweep, 2 * sweep + 1)
                    et_dep = {}
                    inst = None
                    for g2 in range(ngrp):
                        sl = g2 % 2
                        stream(g2, sl, ch, heads[0])
                        wait(t_, krd_dep[sl], d_q)
                        if dbg == 5 and i == 0 and sweep == 0 and g2 <= 1:
                            dgrp().inc(sy.dma_start(
                                out=dbg3_out[:, g2 * QB:(g2 + 1) * QB], in_=krd[sl]), 16)
                            dgrp().inc(sy.dma_start(
                                out=dbg3_out[:, 2 * QB + g2 * KTB * 130:
                                             2 * QB + (g2 + 1) * KTB * 130],
                                in_=vrd[sl]), 16)
                        for kt0 in range(0, KTB, 2):
                            kts = list(range(kt0, min(kt0 + 2, KTB)))
                            g0 = g2 * KTB + kt0
                            w_extra = ([pbdep[2], pbdep[3], pbdep[4], pbdep[5]]
                                       if g0 == 0 else [])
                            wait(t_, et_dep.get(g0 - 2),
                                 et_dep.get(g0 + len(kts) - 3), *w_extra)
                            d_sc = {}
                            for kt in kts:
                                gkt = g2 * KTB + kt
                                sb = 2 + 2 * (gkt % 2)
                                for hf in range(2):
                                    inst = t_.matmul(
                                        pb[sb + hf][:, 0:QB],
                                        krd[sl][64 * hf:64 * hf + 64,
                                                kt * P:kt * P + P],
                                        qs[64 * hf:64 * hf + 64,
                                           ch * QB:(ch + 1) * QB],
                                        start=True, stop=True)
                                d_sc[gkt] = sp.inc(inst)
                            wait(s_, d_sc[g0 + len(kts) - 1],
                                 prev_hv if g0 <= 1 else None)
                            inst = s_.activation(
                                etb.rearrange("p (k q) -> p k q", q=QB)
                                [:, 0:len(kts) * 2, :],
                                pb2345_t[:, 0:2048]
                                .rearrange("p (k q) -> p k q", q=512)
                                [:, 0:len(kts) * 2, 0:QB],
                                AF.Exp, bias=cst(-cshift), scale=cst(0.125))
                            d_e = sa.inc(inst)
                            for b in range(2, 6):
                                pbdep[b] = d_e
                            d_es_all = [d_e]
                            for kt in kts:
                                gkt = g2 * KTB + kt
                                dd = d_e
                                if causal and gkt >= (2 * i) * KTB:
                                    j = gkt - (2 * i) * KTB
                                    wait(v_, d_e)
                                    for hf in range(2):
                                        eslot = et[(gkt % 2) * 2 + hf]
                                        inst = v_.tensor_tensor(eslot, eslot,
                                                                masks[j], ALU.mult)
                                    dd = sv.inc(inst)
                                    d_es_all.append(dd)
                                et_dep[gkt] = dd
                            wait(t_, *d_es_all)
                            for kt in kts:
                                gkt = g2 * KTB + kt
                                if gkt == 0:
                                    wait(t_, pbdep[0], pbdep[1])
                                for hf in range(2):
                                    inst = t_.matmul(
                                        pb[hf][0:65, 0:QB],
                                        vrd[sl][:, kt * 130 + 65 * hf:
                                                kt * 130 + 65 * hf + 65],
                                        et[(gkt % 2) * 2 + hf],
                                        start=(gkt == 0), stop=(gkt == nkt - 1))
                        krd_free[sl] = sp.inc(inst)
                    d_po = krd_free[(ngrp - 1) % 2]
                    if dbg == 4 and i == 0 and sweep == 0:
                        dsb = st["dbg3_sb"]
                        wait(v_, d_po)
                        dv0 = sv.inc(v_.tensor_copy(dsb[0:65, 0:QB], pb[0][0:65, 0:QB]))
                        dv1 = sv.inc(v_.tensor_copy(dsb[0:65, QB:2 * QB], pb[1][0:65, 0:QB]))
                        wait(g_, dv0, dv1)
                        dgrp(sw=True).inc(g_.dma_start(out=dbg3_out[0:65, 0:2 * QB], in_=dsb[0:65, 0:2 * QB]), 16)
                    bcev2 = et[2]
                    wait(v_, d_po)
                    v_.reciprocal(ROWS[64:65, 0:QB], pb[0][64:65, 0:QB])
                    inst = v_.reciprocal(ROWS[64:65, QB:2 * QB],
                                         pb[1][64:65, 0:QB])
                    v_.drain()
                    d_r = sv.inc(inst)
                    wait(t_, d_r, pbdep[6], pbdep[2], d_one)
                    inst = t_.matmul(pb[6][0:64, 0:QB], ONEf[64:65, 0:64],
                                     ROWS[64:65, 0:QB], start=True, stop=True)
                    d_b0 = sp.inc(inst)
                    inst = t_.matmul(pb[2][0:64, 0:QB], ONEf[64:65, 0:64],
                                     ROWS[64:65, QB:2 * QB], start=True, stop=True)
                    d_b1 = sp.inc(inst)
                    d_bc0 = evac_act(6, bcev[0:64, :], dep=d_b0, rows=(0, 64))
                    d_bc1 = evac_act(2, bcev2[0:64, :], dep=d_b1, rows=(0, 64))
                    for hf in range(2):
                        bce = bcev if hf == 0 else bcev2
                        wait(v_, d_bc0 if hf == 0 else d_bc1)
                        inst = v_.tensor_tensor(
                            hva[0:64, heads[hf] * QB:(heads[hf] + 1) * QB],
                            pb[hf][0:64, 0:QB], bce[0:64, :], ALU.mult)
                        d_hv = sv.inc(inst)
                        pbdep[hf] = d_hv
                        hv_dep[heads[hf]] = d_hv
                        prev_hv = d_hv
                for m in range(NCH):
                    bi = 2 + (m % 2)
                    wo_ap, d_wo = wo_load(m, wo_use[m % 2])
                    wait(t_, pbdep[bi], d_wo, *hv_dep)
                    for h in range(NH):
                        inst = t_.matmul(pb[bi][:, 0:QB],
                                         wo_ap[0:64, h * P:h * P + P],
                                         hva[0:64, h * QB:(h + 1) * QB],
                                         start=(h == 0), stop=(h == NH - 1))
                    d_o = sp.inc(inst)
                    wo_use[m % 2] = d_o
                    sl2 = m % 2
                    d_ev = evac_act(bi, scr2[:, sl2 * QB:(sl2 + 1) * QB],
                                    bias=(bo_ap(m) if bo_ap else 0.0),
                                    dep=d_o, war=scr_free[sl2])
                    if dbg == 7 and i == 0 and m == 0:
                        wait(g_, d_ev)
                        dgrp(sw=True).inc(g_.dma_start(out=dbg3_out[:, 0:QB],
                                                in_=scr2[:, 0:QB]), 16)
                        dgrp(sw=True).inc(g_.dma_start(out=dbg3_out[0:HD, QB:QB + NH * P],
                                                in_=wo_ap[0:HD, :]), 16)
                    wait(v_, d_ev, x_guard if i == 0 else None)
                    inst = v_.tensor_tensor(Xc(m, i * QB, QB), Xc(m, i * QB, QB),
                                            scr2[:, sl2 * QB:(sl2 + 1) * QB], ALU.add)
                    scr_free[sl2] = sv.inc(inst)
                if dbg in (3, 7) and i == 0:
                    wait(g_, *hv_dep)
                    dgrp(sw=True).inc(g_.dma_start(out=dbg2_out[0:64, 0:NH * QB],
                                            in_=hva[0:64, 0:NH * QB]), 16)
                hva_free = sp.now()
                qstate["free"] = sp.now()
            return sv.now()

        def postln(gamma_ap=None, rms=False, scb=None, scrl=None):
            d_prev = sv.now()
            d_apply = None
            R = ROWS
            d_casts = {}

            def cast_tb(tb):
                wait(s_, d_prev)
                inst = s_.activation(
                    XBF.rearrange("p (c t) -> p c t", t=T)
                    [:, :, tb * QB:(tb + 1) * QB],
                    X.rearrange("p (c t) -> p c t", t=T)
                    [:, :, tb * QB:(tb + 1) * QB], AF.Copy)
                d_casts[tb] = sa.inc(inst)

            cast_tb(0)
            for tb in range(NQB):
                d_cast = d_casts[tb]
                wait(t_, d_cast, pbdep[6])
                for c in range(NCH):
                    inst = t_.matmul(pb[6][0:1, 0:QB], ONEb[:, 0:1],
                                     XBFc(c, tb * QB, QB),
                                     start=(c == 0), stop=(c == NCH - 1))
                d_s1 = sp.inc(inst)
                d_sq = []
                for c in range(NCH):
                    wait(s_, d_cast if c == 0 else None)
                    inst = s_.activation(scb[:, c * QB:(c + 1) * QB],
                                         XBFc(c, tb * QB, QB), AF.Square)
                    d_sq.append(sa.inc(inst))
                wait(t_, pbdep[5], *d_sq)
                for c in range(NCH):
                    inst = t_.matmul(pb[5][0:1, 0:QB], ONEb[:, 0:1],
                                     scb[:, c * QB:(c + 1) * QB],
                                     start=(c == 0), stop=(c == NCH - 1))
                d_s2 = sp.inc(inst)
                r_mean = R[0:1, 0:QB]
                r_ms = R[0:1, QB:2 * QB]
                r_tmp = R[0:1, 2 * QB:3 * QB]
                r_rstd = R[0:1, 3 * QB:4 * QB]
                r_mneg = R[0:1, 4 * QB:5 * QB]
                wait(s_, d_s1, d_s2)
                s_.activation(r_mean, pb[6][0:1, 0:QB], AF.Copy, scale=1.0 / D)
                inst = s_.activation(r_ms, pb[5][0:1, 0:QB], AF.Copy, scale=1.0 / D)
                s_.drain()
                d_st = sa.inc(inst)
                pbdep[6] = d_st
                pbdep[5] = d_st
                if rms:
                    wait(s_, d_st)
                    inst = s_.activation(r_tmp, r_ms, AF.Sqrt, bias=cst(EPS)[0:1, :])
                    s_.drain()
                    d_sd2 = sa.inc(inst)
                    wait(v_, d_sd2)
                    inst = v_.reciprocal(r_rstd, r_tmp)
                    v_.drain()
                    d_rs = sv.inc(inst)
                else:
                    wait(v_, d_st)
                    v_.tensor_tensor(r_tmp, r_mean, r_mean, ALU.mult)
                    v_.drain()
                    inst = v_.tensor_tensor(r_tmp, r_ms, r_tmp, ALU.subtract)
                    v_.drain()
                    d_var = sv.inc(inst)
                    wait(s_, d_var)
                    inst = s_.activation(r_ms, r_tmp, AF.Sqrt, bias=cst(EPS)[0:1, :])
                    s_.drain()
                    d_sd2 = sa.inc(inst)
                    wait(v_, d_sd2)
                    v_.reciprocal(r_rstd, r_ms)
                    v_.drain()
                    v_.tensor_tensor(r_mneg, r_mean, r_rstd, ALU.mult)
                    v_.drain()
                    inst = v_.tensor_scalar(r_mneg, r_mneg, cst(-1.0)[0:1, :], None, ALU.mult)
                    v_.drain()
                    d_rs = sv.inc(inst)
                wait(t_, d_rs, pbdep[6], pbdep[7], d_one)
                inst = t_.matmul(pb[6][:, 0:QB], ONEf[0:1, 0:P], r_rstd,
                                 start=True, stop=True)
                if not rms:
                    inst = t_.matmul(pb[5][:, 0:QB], ONEf[0:1, 0:P], r_mneg,
                                     start=True, stop=True)
                d_b = sp.inc(inst)
                def apply2(c, sl):
                    if rms:
                        inst = v_.tensor_scalar(XBFc(c, tb * QB, QB),
                                                R[:, sl * QB:(sl + 1) * QB],
                                                gamma_ap(c), None, ALU.mult)
                    else:
                        inst = v_.tensor_tensor(Xc(c, tb * QB, QB),
                                                R[:, sl * QB:(sl + 1) * QB],
                                                pb[5][:, 0:QB], ALU.add)
                    return sv.inc(inst)

                prev = None
                for c in range(NCH):
                    sl = c % 2
                    wait(v_, d_b if c == 0 else None)
                    inst = v_.tensor_tensor(R[:, sl * QB:(sl + 1) * QB],
                                            Xc(c, tb * QB, QB),
                                            pb[6][:, 0:QB], ALU.mult)
                    sv.inc(inst)
                    if prev is not None:
                        d_apply = apply2(prev, 1 - sl)
                    prev = c
                v_.drain()
                d_apply = apply2(prev, prev % 2)
                pbdep[6] = d_apply
                pbdep[5] = d_apply
                if tb + 1 < NQB:
                    cast_tb(tb + 1)
            if not rms:
                wait(s_, d_apply)
                d_apply = sa.inc(s_.activation(XBF[:], X[:], AF.Copy))
            return d_apply

        def l0_attn():
            pt = XPTS.index("l0a")
            m0, mf = abf.mark(), af32.mark()
            bia_ap, d_b1 = load_bia(["l0a_bq", "l0a_bqc", "l0a_bqr", "l0a_bqrs",
                                     "l0a_bkv", "l0a_bkr", "l0a_bkrs", "l0a_bkc",
                                     "l0a_bo"])
            rtc, d_b2 = load_f32("l0a_cos")
            rts, d_b3 = load_f32("l0a_sin")
            bvb, d_b4 = load_f32("l0a_bvb")
            d_aux = [d_b1, d_b2, d_b3, d_b4]
            ckv = abf.alloc(2 * QB * 2)
            scr = af32.alloc(5 * QB)
            kx_sem = [dgrp(sw=True), dgrp(sw=True)]
            vx_sem = [dgrp(sw=True), dgrp(sw=True)]
            d_xbf = st["xbf"]
            ck_free = [None, None]
            st_free = [None, None]
            vst_free = [None, None]

            mA = abf.mark()
            wkv_w, dw1 = load_w("l0a_wkv")
            wkr_w, dw2 = load_w("l0a_wkr")
            wkrs_w, dw3 = load_w("l0a_wkrs")
            wkc_w, dw4 = load_w("l0a_wkc")
            wv_w, dw5 = load_w("l0a_wv")
            kst = abf.alloc(NCH * QB * 2)
            vst = abf.alloc(KTB * VW * 2)
            d_vini = sv.inc(v_.memset(vst[:], 1.0))
            wq_w = wqr_w = wqrs_w = wqc_w = None
            wait(s_, *d_aux)
            wait(v_, *d_aux, d_konst)
            wait(t_, dw1, dw2, dw3, dw4, dw5, d_xbf)
            d_w = None

            def qk_path(tb, is_k, out8, slot_war):
                r = tb % 2
                ck = ckv[:, r * 2 * QB:(r + 1) * 2 * QB]
                rhs = [XBFc(c, tb * QB, QB) for c in range(NCH)]
                wmain = wkv_w if is_k else wq_w
                wrot = wkr_w if is_k else wqr_w
                wrots = wkrs_w if is_k else wqrs_w
                wcc = wkc_w if is_k else wqc_w
                bm = "l0a_bkv" if is_k else "l0a_bq"
                br = "l0a_bkr" if is_k else "l0a_bqr"
                brs = "l0a_bkrs" if is_k else "l0a_bqrs"
                bc = "l0a_bkc" if is_k else "l0a_bqc"
                d_ck = []
                for m in range(2):
                    dmm = proj(m, wmain, 8, 2, m, rhs, [d_w, ck_free[r]])
                    d_ck.append(evac_act(m, ck[:, m * QB:(m + 1) * QB], AF.Gelu,
                                         bias=bia_ap(bm, m), dep=dmm))
                dmm = proj(2, wrot, 8, 1, 0, rhs, [d_w])
                d_kr = evac_act(2, scr[:, 0 * QB:1 * QB], AF.Copy,
                                bias=bia_ap(br), dep=dmm)
                dmm = proj(3, wrots, 8, 1, 0, rhs, [d_w])
                d_krs = evac_act(3, scr[:, 1 * QB:2 * QB], AF.Copy,
                                 bias=bia_ap(brs), dep=dmm)
                wait(v_, d_kr, d_krs)
                v_.tensor_tensor(scr[:, 2 * QB:3 * QB], scr[:, 0 * QB:1 * QB],
                                 rtc[:, tb * QB:(tb + 1) * QB], ALU.mult)
                v_.tensor_tensor(scr[:, 3 * QB:4 * QB], scr[:, 1 * QB:2 * QB],
                                 rts[:, tb * QB:(tb + 1) * QB], ALU.mult)
                v_.drain()
                inst = v_.tensor_tensor(scr[:, 4 * QB:5 * QB], scr[:, 2 * QB:3 * QB],
                                        scr[:, 3 * QB:4 * QB], ALU.add)
                d_rope = sv.inc(inst)
                for m in range(7):
                    bi = 4 + (m % 2)
                    dmm = proj(bi, wcc, 2, 7, m, [ck[:, 0:QB], ck[:, QB:2 * QB]],
                               [d_w] + d_ck)
                    evac_act(bi, out8[:, m * QB:(m + 1) * QB], AF.Gelu,
                             bias=bia_ap(bc, m), dep=dmm, war=slot_war)
                wait(s_, d_rope, slot_war)
                inst = s_.activation(out8[:, 7 * QB:8 * QB], scr[:, 4 * QB:5 * QB],
                                     AF.Gelu)
                d_last = sa.inc(inst)
                return ck, d_ck, d_last

            xw = []
            for tb in range(NQB):
                r = tb % 2
                kst_r = kst[:, r * NCH * QB:(r + 1) * NCH * QB]
                vst_r = vst[:, r * KTB * VW:(r + 1) * KTB * VW]
                ck, d_ck, d_kst = qk_path(tb, True, kst_r, st_free[r])
                d_v = []
                for n in range(2):
                    bi = 5 + n
                    for tt in range(KTB):
                        wait(t_, pbdep[bi], d_ck[0], d_ck[1])
                        for kc in range(2):
                            inst = t_.matmul(
                                pb[bi][:, 0:512],
                                ck[:, kc * QB + tt * P:kc * QB + tt * P + P],
                                wv_w[:, (kc * 2 + n) * 512:(kc * 2 + n) * 512 + 512],
                                start=(kc == 0), stop=(kc == 1))
                        d_mmv = sp.inc(inst)
                        wait(v_, d_mmv, d_vini, vst_free[r])
                        dst = vst_r[:, tt * VW + n * 520: tt * VW + n * 520 + 520] \
                            .rearrange("p (h w) -> p h w", w=65)[:, :, 0:64]
                        inst = v_.tensor_tensor(
                            dst,
                            pb[bi][:, 0:512].rearrange("p (h w) -> p h w", w=64),
                            bvb[:, n * 512:(n + 1) * 512]
                            .rearrange("p (h w) -> p h w", w=64),
                            ALU.add)
                        d_vv = sv.inc(inst)
                        pbdep[bi] = d_vv
                        d_v.append(d_vv)
                if dbg and tb == 0:
                    wait(g_, d_kst)
                    dgrp(sw=True).inc(g_.dma_start(out=dbg_out[0][:, 0:NCH * QB], in_=kst_r), 16)
                    wait(g_, *d_v)
                    dgrp(sw=True).inc(g_.dma_start(
                        out=dbg_out[1][:, 0:KTB * VW], in_=vst_r), 16)
                dk = xwrite_k(pt, kst_r.rearrange("p (c t) -> p c t", t=QB), tb,
                              [d_kst], kx_sem[r])
                dv = xwrite_v(pt, vst_r.rearrange("p (n w) -> p n w", w=VW), tb,
                              d_v, vx_sem[r])
                xw += [dk, dv]
                st_free[r] = dk
                vst_free[r] = dv
                ck_free[r] = sp.now()
            xflag(pt, xw)
            d_ready = xpoll(pt)

            abf.reset(mA)
            wq_w, dq1 = load_w("l0a_wq")
            wqr_w, dq2 = load_w("l0a_wqr")
            wqrs_w, dq3 = load_w("l0a_wqrs")
            wqc_w, dq4 = load_w("l0a_wqc")
            d_wq = [dq1, dq2, dq3, dq4]
            wait(t_, *d_wq)
            mask_t = scb_ln
            d_mask = dgrp(sw=True).inc(
                g_.dma_start(out=mask_t[:, 0:2 * KTB * QB]
                             .rearrange("p (a q) -> p a q", q=QB),
                             in_=W["l0a_mask"].rearrange("a p q -> p a q")), 16)
            masks = [mask_t[:, j * QB:(j + 1) * QB] for j in range(2 * KTB)]
            wait(v_, d_mask)
            qsb = abf.alloc(NCH * QB)
            krd = [abf.alloc(QB) for _ in range(2)]
            vrd = [abf.alloc(KTB * 130) for _ in range(2)]
            etb = abf.alloc(4 * QB)
            et = [etb[:, k * QB:(k + 1) * QB] for k in range(4)]
            bcev = et[3]
            st['dbg3_sb2'] = abf.alloc(4 * QB) if dbg == 6 else None
            st["dbg3_sb"] = af32.alloc(2 * QB) if dbg == 4 else None
            hva = abf.alloc(NH * QB)
            wo_ring = [abf.alloc(NH * P) for _ in range(2)]
            scr2 = af32.alloc(2 * QB)
            d_xchg = xw[-1]
            qstate = {"free": None}
            wait(sy, d_ready)

            def qs_make(i):
                _, _, d_last = qk_path(i, False, qsb, qstate["free"])
                if dbg and i == 0:
                    wait(g_, d_last)
                    dgrp(sw=True).inc(g_.dma_start(out=dbg_out[2][:, 0:NCH * QB], in_=qsb), 16)
                return qsb, d_last

            wo_sems = [dgrp(sw=True), dgrp(sw=True)]

            def wo_load(m, prev_use):
                sl = m % 2
                wait(g_, prev_use)
                inst = g_.dma_start(
                    out=wo_ring[sl][0:HD, :].rearrange("p (a c) -> p a c", c=P),
                    in_=W["l0a_wo"][m * NH:(m + 1) * NH].rearrange("a p c -> p a c"))
                return wo_ring[sl], wo_sems[sl].inc(inst, 16)

            attention_mh(pt, True, C_L0A, wo_load, lambda m: bia_ap("l0a_bo", m),
                         qs_make, masks, krd, vrd, et, etb, bcev, hva, scr2,
                         qstate, d_start=d_xchg)
            abf.reset(m0)
            af32.reset(mf)

        def l0_ffn():
            block_barrier()
            pt = XPTS.index("l0f")
            m0, mf = abf.mark(), af32.mark()
            gam_ap, d_g = load_bia(["l0f_g1", "l0f_g2"])
            wait(v_, d_g)
            d_rms1 = postln(gamma_ap=lambda c: gam_ap("l0f_g1", c), rms=True,
                            scb=scb_ln, scrl=scrl_ln)
            wait(g_, sv.now())
            d_spill = dgrp(sw=True).inc(g_.dma_start(out=spill, in_=X[:]), 16)

            mA = abf.mark()
            wv_w, dk2 = load_w("l0f_wv")
            kst = abf.alloc(NCH * QB)
            vst = abf.alloc(KTB * VW * 2)
            pring = [abf.alloc(8 * P) for _ in range(2)]
            pr_sems = [dgrp(sw=True), dgrp(sw=True)]
            pr_free = [None, None]
            d_vini = sv.inc(v_.memset(vst[:], 1.0))
            wait(t_, dk2, d_rms1)
            kx_sem = [dgrp(sw=True), dgrp(sw=True)]
            vx_sem = [dgrp(sw=True), dgrp(sw=True)]
            st_free = [None, None]
            vst_free = [None, None]
            xw = []

            def plain_path(tb, wname, out8, slot_war):
                rhs = [XBFc(c, tb * QB, QB) for c in range(NCH)]
                d_last = None
                for m in range(NCH):
                    wsl = m % 2
                    wait(g_, pr_free[wsl])
                    inst = g_.dma_start(
                        out=pring[wsl].rearrange("p (a c) -> p a c", c=P),
                        in_=W[wname][m * 8:(m + 1) * 8].rearrange("a p c -> p a c"))
                    d_w = pr_sems[wsl].inc(inst, 16)
                    bi = m % 4
                    dmm = proj(bi, pring[wsl], 8, 1, 0, rhs, [d_w])
                    pr_free[wsl] = dmm
                    d_last = evac_act(bi, out8[:, m * QB:(m + 1) * QB], AF.Copy,
                                      dep=dmm, war=slot_war)
                return d_last

            for tb in range(NQB):
                r = tb % 2
                kst_r = kst
                vst_r = vst[:, r * KTB * VW:(r + 1) * KTB * VW]
                d_kst = plain_path(tb, "l0f_wk", kst_r, st_free[0])
                d_v = []
                for n in range(2):
                    bi = 4 + n
                    for tt in range(KTB):
                        wait(t_, pbdep[bi])
                        for kc in range(NCH):
                            inst = t_.matmul(
                                pb[bi][:, 0:512],
                                XBFc(kc, tb * QB + tt * P, P),
                                wv_w[:, (kc * 2 + n) * 512:(kc * 2 + n) * 512 + 512],
                                start=(kc == 0), stop=(kc == NCH - 1))
                        d_mmv = sp.inc(inst)
                        wait(v_, d_mmv, d_vini, vst_free[r])
                        dst = vst_r[:, tt * VW + n * 520: tt * VW + n * 520 + 520] \
                            .rearrange("p (h w) -> p h w", w=65)[:, :, 0:64]
                        inst = v_.tensor_copy(
                            dst, pb[bi][:, 0:512].rearrange("p (h w) -> p h w", w=64))
                        d_vv = sv.inc(inst)
                        pbdep[bi] = d_vv
                        d_v.append(d_vv)
                dk = xwrite_k(pt, kst_r.rearrange("p (c t) -> p c t", t=QB), tb,
                              [d_kst], kx_sem[r])
                dv = xwrite_v(pt, vst_r.rearrange("p (n w) -> p n w", w=VW), tb,
                              d_v, vx_sem[r])
                xw += [dk, dv]
                st_free[r] = dk
                vst_free[r] = dv
            xflag(pt, xw)
            d_ready = xpoll(pt)

            abf.reset(mA)
            pring = [abf.alloc(8 * P) for _ in range(2)]
            pr_free = [None, None]
            qsb = abf.alloc(NCH * QB)
            krd = [abf.alloc(QB) for _ in range(2)]
            vrd = [abf.alloc(KTB * 130) for _ in range(2)]
            etb = abf.alloc(4 * QB)
            et = [etb[:, k * QB:(k + 1) * QB] for k in range(4)]
            bcev = et[3]
            hva = abf.alloc(NH * QB)
            wo_ring = [abf.alloc(NH * P) for _ in range(2)]
            scr2 = af32.alloc(2 * QB)
            d_xchg = xw[-1]
            qstate = {"free": None}
            wait(sy, d_ready)
            wo_sems = [dgrp(sw=True), dgrp(sw=True)]

            def qs_make(i):
                d_last = plain_path(i, "l0f_wq", qsb, qstate["free"])
                return qsb, d_last

            def wo_load(m, prev_use):
                sl = m % 2
                wait(g_, prev_use)
                inst = g_.dma_start(
                    out=wo_ring[sl][0:HD, :].rearrange("p (a c) -> p a c", c=P),
                    in_=W["l0f_wo"][m * NH:(m + 1) * NH].rearrange("a p c -> p a c"))
                return wo_ring[sl], wo_sems[sl].inc(inst, 16)

            attention_mh(pt, False, C_L0F, wo_load, None, qs_make, None,
                         krd, vrd, et, etb, bcev, hva, scr2, qstate,
                         d_start=d_xchg, x_guard=d_spill)
            abf.reset(mA)

            d_xn2 = postln(gamma_ap=lambda c: gam_ap("l0f_g2", c), rms=True,
                           scb=scb_ln, scrl=scrl_ln)
            wait(t_, d_xn2)

            oh8_sb, d_oh = load_w("ident8")
            idt_sb, d_id = load_w("identt")
            gate_w, d_gt = load_w("l0f_gate")
            weT = abf.alloc(T)
            bcw = [abf.alloc(QB) for _ in range(2)]
            sgb = abf.alloc(QB * 2)
            usb = abf.alloc(QB * 2)
            hsb = abf.alloc(24 * QB)
            wring = [abf.alloc(32 * P) for _ in range(2)]
            webf = abf.alloc(16)

            def hsb_ap(j, fc):
                s = j * 16 + fc
                return (scb_ln[:, s * QB:(s + 1) * QB] if s < 8
                        else hsb[:, (s - 8) * QB:(s - 7) * QB])
            rsc = af32.alloc(64)
            spl_f = af32.alloc(2 * QB)
            wr_sems = [dgrp(sw=True), dgrp(sw=True)]
            wr_free = [None, None]
            spl_sems = [dgrp(), dgrp()]
            wait(t_, d_oh, d_id, d_gt)

            lg_free = None
            d_weT = None
            for tt in range(T // P):
                bi = tt % 2
                wait(t_, pbdep[bi])
                for kc in range(NCH):
                    inst = t_.matmul(pb[bi][:, 0:8],
                                     XBFc(kc, tt * P, P),
                                     gate_w[:, kc * 8:(kc + 1) * 8],
                                     start=(kc == 0), stop=(kc == NCH - 1))
                d_lg = sp.inc(inst)
                lg = rsc[:, 0:8]
                d_ev = evac_act(bi, lg, dep=d_lg, war=lg_free, n=8)
                m1 = rsc[:, 8:9]
                z2 = rsc[:, 16:24]
                m2 = rsc[:, 9:10]
                e12 = rsc[:, 10:11]
                p1 = rsc[:, 11:12]
                p2 = rsc[:, 12:13]
                eq1 = rsc[:, 24:32]
                eq2 = rsc[:, 32:40]
                wef = rsc[:, 40:48]
                wait(v_, d_ev)
                v_.tensor_reduce(m1, lg, mybir.AxisListType.X, ALU.max)
                v_.drain()
                v_.tensor_scalar(eq1, lg, m1, None, ALU.is_ge)
                v_.drain()
                v_.tensor_scalar(z2, eq1, cst(-1e9), None, ALU.mult)
                v_.drain()
                v_.tensor_tensor(z2, lg, z2, ALU.add)
                v_.drain()
                v_.tensor_reduce(m2, z2, mybir.AxisListType.X, ALU.max)
                v_.drain()
                inst = v_.tensor_tensor(e12, m2, m1, ALU.subtract)
                v_.drain()
                d_d = sv.inc(inst)
                wait(s_, d_d)
                inst = s_.activation(e12, e12, AF.Exp)
                s_.drain()
                d_e = sa.inc(inst)
                wait(v_, d_e)
                v_.tensor_scalar(p2, e12, cst(1.0), None, ALU.add)
                v_.drain()
                v_.reciprocal(p1, p2)
                v_.drain()
                v_.tensor_tensor(p2, e12, p1, ALU.mult)
                v_.drain()
                v_.tensor_scalar(eq2, z2, m2, None, ALU.is_ge)
                v_.drain()
                v_.tensor_scalar(eq1, eq1, p1, None, ALU.mult)
                v_.tensor_scalar(eq2, eq2, p2, None, ALU.mult)
                v_.drain()
                v_.tensor_tensor(wef, eq1, eq2, ALU.add)
                v_.drain()
                inst = v_.tensor_copy(webf[:, 0:8], wef)
                v_.drain()
                d_web = sv.inc(inst)
                lg_free = d_web
                wait(t_, d_web, pbdep[7])
                inst = t_.transpose(pbT[0:8, 0:P], webf[:, 0:8], idt_sb)
                d_tr = sp.inc(inst)
                wait(s_, d_tr)
                inst = s_.activation(weT[0:8, tt * P:(tt + 1) * P], pbT[0:8, 0:P],
                                     AF.Copy)
                d_weT = sa.inc(inst)
                pbdep[7] = d_weT

            bc_free = [None, None]
            h_free = None
            sg_free = [None, None]
            us_free = [None, None]
            for p0 in range(0, NQB, 2):
                tbs = tuple(range(p0, min(p0 + 2, NQB)))
                rhs2 = [[XBFc(c, tb * QB, QB) for c in range(NCH)] for tb in tbs]
                for e in range(8):
                    d_bce = []
                    for j, tb in enumerate(tbs):
                        wait(t_, pbdep[6], d_weT, bc_free[j])
                        inst = t_.matmul(pb[6][:, 0:QB],
                                         oh8_sb[0:8, e * P:(e + 1) * P],
                                         weT[0:8, tb * QB:(tb + 1) * QB],
                                         start=True, stop=True)
                        d_bc = sp.inc(inst)
                        d_bce.append(evac_act(6, bcw[j], dep=d_bc,
                                              war=bc_free[j]))
                    d_h = [[], []]
                    d_us_last = [None, None]
                    for fcp in range(8):
                        wsl = fcp % 2
                        wait(g_, wr_free[wsl])
                        inst = g_.dma_start(
                            out=wring[wsl][:, 0:32 * P]
                            .rearrange("p (x c) -> p x c", c=P),
                            in_=W["l0f_w13"][e, fcp]
                            .rearrange("s a p c -> p (s a) c"))
                        d_w = wr_sems[wsl].inc(inst, 16)
                        d_mm_last = None
                        for fi in range(2):
                            fc = 2 * fcp + fi
                            for j, tb in enumerate(tbs):
                                bi = j
                                bj = 2 + j
                                dmm = proj(bi, wring[wsl][:, (2 * fi) * 8 * P:
                                                          (2 * fi + 1) * 8 * P],
                                           8, 1, 0, rhs2[j],
                                           [d_w] if fi + j == 0 else [])
                                d_sg = evac_act(bi, sgb[:, j * QB:(j + 1) * QB],
                                                AF.Silu, dep=dmm, war=sg_free[j])
                                dmm2 = proj(bj, wring[wsl][:, (2 * fi + 1) * 8 * P:
                                                           (2 * fi + 2) * 8 * P],
                                            8, 1, 0, rhs2[j], [])
                                d_mm_last = dmm2
                                wait(v_, dmm2, d_bce[j], d_sg, h_free, us_free[j])
                                inst = v_.tensor_tensor(
                                    usb[:, j * QB:(j + 1) * QB],
                                    pb[bj][:, 0:QB], bcw[j], ALU.mult)
                                d_us = sv.inc(inst)
                                pbdep[bj] = d_us
                                d_us_last[j] = d_us
                                v_.drain()
                                inst = v_.tensor_tensor(
                                    hsb_ap(j, fc), sgb[:, j * QB:(j + 1) * QB],
                                    usb[:, j * QB:(j + 1) * QB], ALU.mult)
                                dh = sv.inc(inst)
                                d_h[j].append(dh)
                                sg_free[j] = dh
                                us_free[j] = dh
                        wr_free[wsl] = d_mm_last
                    bc_free = [d_us_last[0], d_us_last[1]]
                    for mp in range(4):
                        wsl2 = mp % 2
                        wait(g_, wr_free[wsl2])
                        inst = g_.dma_start(
                            out=wring[wsl2][:, 0:32 * P]
                            .rearrange("p (x c) -> p x c", c=P),
                            in_=W["l0f_w2p"][e, mp]
                            .rearrange("f a p c -> p (f a) c"))
                        d_w2 = wr_sems[wsl2].inc(inst, 16)
                        d_o = None
                        for mi in range(2):
                            m = 2 * mp + mi
                            for j, tb in enumerate(tbs):
                                bi = 4 + j
                                wait(t_, pbdep[bi],
                                     d_w2 if mi + j == 0 else None,
                                     *(d_h[j] if mp == 0 and mi == 0 else []))
                                for fc in range(16):
                                    inst = t_.matmul(
                                        pb[bi][:, 0:QB],
                                        wring[wsl2][:, (mi * 16 + fc) * P:
                                                    (mi * 16 + fc + 1) * P],
                                        hsb_ap(j, fc),
                                        start=(fc == 0), stop=(fc == 15))
                                d_o = sp.inc(inst)
                                wait(v_, d_o)
                                if p0 == 0 and e == 0 and mp == 0:
                                    v_.drain()
                                inst = v_.tensor_tensor(Xc(m, tb * QB, QB),
                                                        Xc(m, tb * QB, QB),
                                                        pb[bi][:, 0:QB], ALU.add)
                                d_x = sv.inc(inst)
                                pbdep[bi] = d_x
                        wr_free[wsl2] = d_o
                    h_free = sp.now()

            v_.drain()
            spf = [None, None]
            for c in range(NCH):
                for tb in range(NQB):
                    r3 = (c * NQB + tb) % 2
                    wait(sy, spf[r3], d_spill)
                    inst = sy.dma_start(
                        out=spl_f[:, r3 * QB:(r3 + 1) * QB],
                        in_=spill[:, c * T + tb * QB:c * T + tb * QB + QB])
                    d_sp2 = spl_sems[r3].inc(inst, 16)
                    wait(v_, d_sp2)
                    inst = v_.tensor_tensor(Xc(c, tb * QB, QB), Xc(c, tb * QB, QB),
                                            spl_f[:, r3 * QB:(r3 + 1) * QB], ALU.add)
                    spf[r3] = sv.inc(inst)
            abf.reset(m0)
            af32.reset(mf)

        def l1_attn():
            block_barrier()
            pt = XPTS.index("l1a")
            m0, mf = abf.mark(), af32.mark()
            bia_ap, d_b = load_bia(["l1a_bq", "l1a_bqs", "l1a_bk", "l1a_bks"])
            bvb, d_bvb = load_f32("l1a_bvb")
            rring = af32.alloc(4 * QB)
            scr = af32.alloc(3 * QB)
            rsem = [dgrp(), dgrp()]
            rfree = [None, None]

            rring2 = [abf.alloc(16 * P) for _ in range(2)]
            rr_sems = [dgrp(sw=True), dgrp(sw=True)]
            rr_free = [None, None]

            def rope_path(tb, wA, wB, bA, bB, out8, slot_war):
                rhs = [XBFc(c, tb * QB, QB) for c in range(NCH)]
                d_last = None
                for m in range(NCH):
                    rsl = m % 2
                    wait(sy, rfree[rsl])
                    i1 = sy.dma_start(out=rring[:, rsl * 2 * QB:rsl * 2 * QB + QB],
                                      in_=W["l1a_cos"][m][:, tb * QB:(tb + 1) * QB])
                    rsem[rsl].inc(i1, 16)
                    i2 = sy.dma_start(
                        out=rring[:, rsl * 2 * QB + QB:rsl * 2 * QB + 2 * QB],
                        in_=W["l1a_sin"][m][:, tb * QB:(tb + 1) * QB])
                    d_rt = rsem[rsl].inc(i2, 16)
                    wait(g_, rr_free[rsl])
                    iw = g_.dma_start(
                        out=rring2[rsl][:, 0:8 * P].rearrange("p (a c) -> p a c", c=P),
                        in_=W[wA][m * 8:(m + 1) * 8].rearrange("a p c -> p a c"))
                    rr_sems[rsl].inc(iw, 16)
                    iw = g_.dma_start(
                        out=rring2[rsl][:, 8 * P:16 * P]
                        .rearrange("p (a c) -> p a c", c=P),
                        in_=W[wB][m * 8:(m + 1) * 8].rearrange("a p c -> p a c"))
                    d_wm = rr_sems[rsl].inc(iw, 16)
                    dmm = proj(m % 2, rring2[rsl][:, 0:8 * P], 8, 1, 0, rhs, [d_wm])
                    d_q = evac_act(m % 2, scr[:, 0:QB], AF.Copy, bias=bia_ap(bA, m),
                                   dep=dmm, war=state.get("l1r0"))
                    dmm2 = proj(2 + m % 2, rring2[rsl][:, 8 * P:16 * P], 8, 1, 0,
                                rhs, [d_wm])
                    rr_free[rsl] = dmm2
                    d_qs = evac_act(2 + m % 2, scr[:, QB:2 * QB], AF.Copy,
                                    bias=bia_ap(bB, m), dep=dmm2,
                                    war=state.get("l1r1"))
                    wait(v_, d_q, d_qs, d_rt, slot_war if m < 2 else None)
                    v_.tensor_tensor(scr[:, 2 * QB:3 * QB], scr[:, 0:QB],
                                     rring[:, rsl * 2 * QB:rsl * 2 * QB + QB],
                                     ALU.mult)
                    v_.drain()
                    state["l1r0"] = sv.inc(v_.nop()) if False else None
                    inst = v_.tensor_tensor(scr[:, 0:QB], scr[:, QB:2 * QB],
                                            rring[:, rsl * 2 * QB + QB:
                                                  rsl * 2 * QB + 2 * QB], ALU.mult)
                    d_t2 = sv.inc(inst)
                    state["l1r1"] = d_t2
                    v_.drain()
                    inst = v_.tensor_tensor(out8[:, m * QB:(m + 1) * QB],
                                            scr[:, 2 * QB:3 * QB], scr[:, 0:QB],
                                            ALU.add)
                    d_last = sv.inc(inst)
                    state["l1r0"] = d_last
                    rfree[rsl] = d_last
                return d_last

            mA = abf.mark()
            kst = abf.alloc(NCH * QB * 2)
            wait(t_, d_b)
            wait(v_, d_b)
            kx_sem = [dgrp(sw=True), dgrp(sw=True)]
            st_free = [None, None]
            xw = []
            for tb in range(NQB):
                r = tb % 2
                kst_r = kst[:, r * NCH * QB:(r + 1) * NCH * QB]
                d_kst = rope_path(tb, "l1a_wk", "l1a_wks", "l1a_bk", "l1a_bks",
                                  kst_r, st_free[r])
                dk = xwrite_k(pt, kst_r.rearrange("p (c t) -> p c t", t=QB), tb,
                              [d_kst], kx_sem[r])
                xw.append(dk)
                st_free[r] = dk
            abf.reset(mA)
            wv_w, dv1 = load_w("l1a_wv")
            vst = abf.alloc(KTB * VW * 2)
            d_vini = sv.inc(v_.memset(vst[:], 1.0))
            wait(t_, dv1)
            vx_sem = [dgrp(sw=True), dgrp(sw=True)]
            vst_free = [None, None]
            for tb in range(NQB):
                r = tb % 2
                vst_r = vst[:, r * KTB * VW:(r + 1) * KTB * VW]
                d_v = []
                for n in range(2):
                    bi = 4 + n
                    for tt in range(KTB):
                        wait(t_, pbdep[bi])
                        for kc in range(NCH):
                            inst = t_.matmul(
                                pb[bi][:, 0:512],
                                XBFc(kc, tb * QB + tt * P, P),
                                wv_w[:, (kc * 2 + n) * 512:(kc * 2 + n) * 512 + 512],
                                start=(kc == 0), stop=(kc == NCH - 1))
                        d_mmv = sp.inc(inst)
                        wait(v_, d_mmv, d_vini, vst_free[r], d_bvb)
                        inst = v_.tensor_tensor(
                            vst_r[:, tt * VW + n * 512: tt * VW + n * 512 + 512],
                            pb[bi][:, 0:512],
                            bvb[:, n * 512:(n + 1) * 512], ALU.add)
                        d_vv = sv.inc(inst)
                        pbdep[bi] = d_vv
                        d_v.append(d_vv)
                dv = xwrite_v(pt, vst_r.rearrange("p (n w) -> p n w", w=VW), tb,
                              d_v, vx_sem[r])
                xw.append(dv)
                vst_free[r] = dv
            xflag(pt, xw)
            d_ready = xpoll(pt)

            abf.reset(mA)
            qsb = abf.alloc(NCH * QB)
            krd = [abf.alloc(NCH * QB) for _ in range(2)]
            vrd = [abf.alloc(KTB * 512) for _ in range(2)]
            etb = abf.alloc(4 * QB)
            et = [etb[:, k * QB:(k + 1) * QB] for k in range(4)]
            bcev = abf.alloc(QB)
            scr2 = af32.alloc(QB)
            wait(sy, d_ready)
            d_xchg = xw[-1]
            krd_sem = [dgrp(), dgrp()]
            vrd_sem = [dgrp(), dgrp()]
            krd_free = [d_xchg, d_xchg]
            vrd_free = [d_xchg, d_xchg]
            qfree = None
            xg = None

            for i in range(NQB):
                d_q = rope_path(i, "l1a_wq", "l1a_wqs", "l1a_bq", "l1a_bqs",
                                qsb, qfree)
                den_saved = None
                for p2 in range(2):
                    et_dep = {}
                    po = (pb[2], pb[3], pb[4], pb[5])
                    inst_pv = None
                    for g2 in range(NBLK):
                        vsl = g2 % 2
                        wait(sy, krd_free[vsl])
                        i1 = sy.dma_start(
                            out=krd[vsl].rearrange("p (c t) -> p c t", t=QB),
                            in_=ksh[pt][g2 % 2].rearrange("p (c t) -> p c t", t=T)
                            [:, :, (g2 // 2) * QB:(g2 // 2 + 1) * QB])
                        d_k = krd_sem[vsl].inc(i1, 16)
                        wait(sy, vrd_free[vsl])
                        i2 = sy.dma_start(
                            out=vrd[vsl][:, 0:KTB * 512]
                            .rearrange("p (n w) -> p n w", w=512),
                            in_=vsh[pt][g2 % 2][(g2 // 2) * KTB:(g2 // 2 + 1) * KTB,
                                                :, p2 * 512:(p2 + 1) * 512]
                            .rearrange("n p w -> p n w"))
                        d_v2 = vrd_sem[vsl].inc(i2, 16)
                        wait(t_, d_k, d_v2, d_q)
                        for kt0 in range(0, KTB, 2):
                            kts = list(range(kt0, min(kt0 + 2, KTB)))
                            g0 = g2 * KTB + kt0
                            w_extra = [pbdep[0], pbdep[1]] if g0 == 0 else []
                            wait(t_, et_dep.get(g0 - 2),
                                 et_dep.get(g0 + len(kts) - 3), *w_extra)
                            d_sc = {}
                            for kt in kts:
                                gkt = g2 * KTB + kt
                                sb = gkt % 2
                                for kc in range(NCH):
                                    inst = t_.matmul(
                                        pb[sb][:, 0:QB],
                                        krd[vsl][:, kc * QB + kt * P:
                                                 kc * QB + kt * P + P],
                                        qsb[:, kc * QB:(kc + 1) * QB],
                                        start=(kc == 0), stop=(kc == NCH - 1))
                                d_sc[gkt] = sp.inc(inst)
                            wait(s_, d_sc[g0 + len(kts) - 1])
                            inst = s_.activation(
                                etb.rearrange("p (k q) -> p k q", q=QB)
                                [:, g0 % 4:g0 % 4 + len(kts), :],
                                pb01_t[:, 0:1024]
                                .rearrange("p (k q) -> p k q", q=512)
                                [:, 0:len(kts), 0:QB],
                                AF.Exp, bias=cst(-C_L1A), scale=cst(1.0 / 32.0))
                            d_e = sa.inc(inst)
                            pbdep[0] = d_e
                            pbdep[1] = d_e
                            for kt in kts:
                                et_dep[g2 * KTB + kt] = d_e
                            wait(t_, d_e)
                            for kt in kts:
                                gkt = g2 * KTB + kt
                                eslot = et[gkt % 4]
                                if gkt == 0:
                                    wait(t_, pbdep[2], pbdep[3], pbdep[4], pbdep[5],
                                         pbdep[6])
                                for dcl in range(4):
                                    inst_pv = t_.matmul(
                                        po[dcl][:, 0:QB],
                                        vrd[vsl][:, kt * 512 + dcl * P:
                                                 kt * 512 + dcl * P + P],
                                        eslot,
                                        start=(gkt == 0),
                                        stop=(gkt == NBLK * KTB - 1))
                                if p2 == 0:
                                    inst_pv = t_.matmul(
                                        pb[6][0:1, 0:QB], ONEb[:, 0:1], eslot,
                                        start=(gkt == 0),
                                        stop=(gkt == NBLK * KTB - 1))
                        d_g2 = sp.inc(inst_pv)
                        krd_free[vsl] = d_g2
                        vrd_free[vsl] = d_g2
                    d_po = krd_free[(NBLK - 1) % 2]
                    if p2 == 0:
                        wait(v_, d_po)
                        inst = v_.reciprocal(ROWS[0:1, 0:QB], pb[6][0:1, 0:QB])
                        v_.drain()
                        d_r = sv.inc(inst)
                        pbdep[6] = d_r
                        wait(t_, d_r, pbdep[0])
                        inst = t_.matmul(pb[0][:, 0:QB], ONEf[0:1, 0:P],
                                         ROWS[0:1, 0:QB], start=True, stop=True)
                        d_bcm = sp.inc(inst)
                        den_saved = evac_act(0, bcev, dep=d_bcm)
                    for dcl in range(4):
                        c = p2 * 4 + dcl
                        wait(v_, d_po, den_saved, xg)
                        inst = v_.tensor_tensor(scr2[:, 0:QB], po[dcl][:, 0:QB],
                                                bcev, ALU.mult)
                        sv.inc(inst)
                        v_.drain()
                        inst = v_.tensor_tensor(Xc(c, i * QB, QB), Xc(c, i * QB, QB),
                                                scr2[:, 0:QB], ALU.add)
                        d_x = sv.inc(inst)
                        pbdep[2 + dcl] = d_x
                qfree = sp.now()
            abf.reset(m0)
            af32.reset(mf)

        def l1_ffn():
            block_barrier()
            m0, mf = abf.mark(), af32.mark()
            oh16_sb, d_oh = load_w("ident16")
            idt_sb, d_id = load_w("identt")
            rt_w, d_rt = load_w("l1f_router")
            weT = abf.alloc(T)
            webf = abf.alloc(16)
            bcw = [abf.alloc(QB) for _ in range(2)]
            sgb = abf.alloc(QB * 2)
            hsb = abf.alloc(32 * QB)
            wring = [abf.alloc(16 * P) for _ in range(2)]
            rsc = af32.alloc(64)
            wr_sems = [dgrp(sw=True), dgrp(sw=True)]
            wr_free = [None, None]
            wait(t_, d_oh, d_id, d_rt)

            def hss_ap(j, s):
                s = j * 16 + s
                return hsb[:, s * QB:(s + 1) * QB]

            def hr_ap(j, s):
                s = j * 8 + s
                return hsb[:, s * QB:(s + 1) * QB]

            def xsc(j, c):
                if j == 0:
                    return scb_ln[:, c * QB:(c + 1) * QB]
                return hsb[:, (16 + c) * QB:(17 + c) * QB]

            lg_free = None
            d_weT = None
            for tt in range(T // P):
                bi = tt % 2
                wait(t_, pbdep[bi])
                for kc in range(NCH):
                    inst = t_.matmul(pb[bi][:, 0:16],
                                     XBFc(kc, tt * P, P),
                                     rt_w[:, kc * 16:(kc + 1) * 16],
                                     start=(kc == 0), stop=(kc == NCH - 1))
                d_lg = sp.inc(inst)
                lg = rsc[:, 0:16]
                m1 = rsc[:, 16:17]
                eq = rsc[:, 32:48]
                sg2 = rsc[:, 48:64]
                d_ev = evac_act(bi, lg, dep=d_lg, war=lg_free, n=16)
                wait(v_, d_ev)
                v_.tensor_reduce(m1, lg, mybir.AxisListType.X, ALU.max)
                v_.drain()
                inst = v_.tensor_scalar(eq, lg, m1, None, ALU.is_ge)
                v_.drain()
                d_eq = sv.inc(inst)
                wait(s_, d_ev)
                inst = s_.activation(sg2, lg, AF.Sigmoid)
                s_.drain()
                d_sg = sa.inc(inst)
                wait(v_, d_sg, d_eq)
                inst = v_.tensor_tensor(eq, eq, sg2, ALU.mult)
                v_.drain()
                inst = v_.tensor_copy(webf[:, 0:16], eq)
                v_.drain()
                d_web = sv.inc(inst)
                lg_free = d_web
                wait(t_, d_web, pbdep[7])
                inst = t_.transpose(pbT[0:16, 0:P], webf[:, 0:16], idt_sb)
                d_tr = sp.inc(inst)
                wait(s_, d_tr)
                inst = s_.activation(weT[0:16, tt * P:(tt + 1) * P],
                                     pbT[0:16, 0:P], AF.Copy)
                d_weT = sa.inc(inst)
                pbdep[7] = d_weT

            sg_free = [None, None]
            h_free = None
            for p0 in range(0, NQB, 2):
                tbs = tuple(range(p0, min(p0 + 2, NQB)))
                rhs2 = [[XBFc(c, tb * QB, QB) for c in range(NCH)] for tb in tbs]
                for half in range(2):
                    d_h = [[] for _ in tbs]
                    for fci in range(16):
                        fc = half * 16 + fci
                        wsl = fci % 2
                        wait(g_, wr_free[wsl])
                        inst = g_.dma_start(
                            out=wring[wsl][:, 0:16 * P]
                            .rearrange("p (x c) -> p x c", c=P),
                            in_=W["l1f_sgu"][fc].rearrange("s a p c -> p (s a) c"))
                        d_w = wr_sems[wsl].inc(inst, 16)
                        d_mm_last = None
                        for j, tb in enumerate(tbs):
                            bi = j
                            bj = 2 + j
                            dmm = proj(bi, wring[wsl][:, 0:8 * P], 8, 1, 0,
                                       rhs2[j], [d_w] if j == 0 else [])
                            d_sg = evac_act(bi, sgb[:, j * QB:(j + 1) * QB],
                                            AF.Silu, dep=dmm, war=sg_free[j])
                            dmm2 = proj(bj, wring[wsl][:, 8 * P:16 * P], 8, 1, 0,
                                        rhs2[j], [])
                            d_mm_last = dmm2
                            wait(v_, dmm2, d_sg, h_free)
                            inst = v_.tensor_tensor(hss_ap(j, fci),
                                                    sgb[:, j * QB:(j + 1) * QB],
                                                    pb[bj][:, 0:QB], ALU.mult)
                            dh = sv.inc(inst)
                            pbdep[bj] = dh
                            d_h[j].append(dh)
                            sg_free[j] = dh
                        wr_free[wsl] = d_mm_last
                    for m in range(NCH):
                        wsl2 = m % 2
                        wait(g_, wr_free[wsl2])
                        inst = g_.dma_start(
                            out=wring[wsl2][:, 0:16 * P]
                            .rearrange("p (x c) -> p x c", c=P),
                            in_=W["l1f_sd"][half * 16:(half + 1) * 16, m]
                            .rearrange("a p c -> p a c"))
                        d_w2 = wr_sems[wsl2].inc(inst, 16)
                        d_o = None
                        for j, tb in enumerate(tbs):
                            bi = 4 + j
                            wait(t_, pbdep[bi], d_w2 if j == 0 else None,
                                 *(d_h[j] if m == 0 else []))
                            for fci in range(16):
                                inst = t_.matmul(
                                    pb[bi][:, 0:QB],
                                    wring[wsl2][:, fci * P:(fci + 1) * P],
                                    hss_ap(j, fci),
                                    start=(fci == 0), stop=(fci == 15))
                            d_o = sp.inc(inst)
                            wait(v_, d_o)
                            inst = v_.tensor_tensor(Xc(m, tb * QB, QB),
                                                    Xc(m, tb * QB, QB),
                                                    pb[bi][:, 0:QB], ALU.add)
                            pbdep[bi] = sv.inc(inst)
                        wr_free[wsl2] = d_o
                    h_free = sp.now()

            bc_free = [None, None]
            xs_free = [h_free, h_free]
            for p0 in range(0, NQB, 2):
                tbs = tuple(range(p0, min(p0 + 2, NQB)))
                for e in range(16):
                    d_xs = [[] for _ in tbs]
                    for j, tb in enumerate(tbs):
                        wait(t_, pbdep[6], d_weT, bc_free[j])
                        inst = t_.matmul(pb[6][:, 0:QB],
                                         oh16_sb[0:16, e * P:(e + 1) * P],
                                         weT[0:16, tb * QB:(tb + 1) * QB],
                                         start=True, stop=True)
                        d_bc = sp.inc(inst)
                        d_bce = evac_act(6, bcw[j], dep=d_bc, war=bc_free[j])
                        for c in range(NCH):
                            wait(v_, d_bce if c == 0 else None,
                                 xs_free[j] if c == 0 else None)
                            inst = v_.tensor_tensor(xsc(j, c),
                                                    XBFc(c, tb * QB, QB), bcw[j],
                                                    ALU.mult)
                            d_xs[j].append(sv.inc(inst))
                        bc_free[j] = d_xs[j][-1]
                    rhs2 = [[xsc(j, c) for c in range(NCH)]
                            for j in range(len(tbs))]
                    d_h = [[] for _ in tbs]
                    for fc in range(8):
                        wsl = fc % 2
                        wait(g_, wr_free[wsl])
                        inst = g_.dma_start(
                            out=wring[wsl][:, 0:16 * P]
                            .rearrange("p (x c) -> p x c", c=P),
                            in_=W["l1f_gu"][e, fc].rearrange("s a p c -> p (s a) c"))
                        d_w = wr_sems[wsl].inc(inst, 16)
                        d_mm_last = None
                        for j, tb in enumerate(tbs):
                            bi = j
                            bj = 2 + j
                            dmm = proj(bi, wring[wsl][:, 0:8 * P], 8, 1, 0,
                                       rhs2[j],
                                       ([d_w] if j == 0 else []) +
                                       (d_xs[j] if fc == 0 else []))
                            d_sg = evac_act(bi, sgb[:, j * QB:(j + 1) * QB],
                                            AF.Silu, dep=dmm, war=sg_free[j])
                            dmm2 = proj(bj, wring[wsl][:, 8 * P:16 * P], 8, 1, 0,
                                        rhs2[j], [])
                            d_mm_last = dmm2
                            wait(v_, dmm2, d_sg, h_free)
                            inst = v_.tensor_tensor(hr_ap(j, fc),
                                                    sgb[:, j * QB:(j + 1) * QB],
                                                    pb[bj][:, 0:QB], ALU.mult)
                            dh = sv.inc(inst)
                            pbdep[bj] = dh
                            d_h[j].append(dh)
                            sg_free[j] = dh
                        wr_free[wsl] = d_mm_last
                    xs_free = [sp.now(), sp.now()]
                    for mp in range(4):
                        wsl2 = mp % 2
                        wait(g_, wr_free[wsl2])
                        inst = g_.dma_start(
                            out=wring[wsl2][:, 0:16 * P]
                            .rearrange("p (x c) -> p x c", c=P),
                            in_=W["l1f_wdp"][e, mp]
                            .rearrange("f a p c -> p (f a) c"))
                        d_w2 = wr_sems[wsl2].inc(inst, 16)
                        d_o = None
                        for mi in range(2):
                            m = 2 * mp + mi
                            for j, tb in enumerate(tbs):
                                bi = 4 + j
                                wait(t_, pbdep[bi],
                                     d_w2 if mi + j == 0 else None,
                                     *(d_h[j] if mp == 0 and mi == 0 else []))
                                for fc in range(8):
                                    inst = t_.matmul(
                                        pb[bi][:, 0:QB],
                                        wring[wsl2][:, (mi * 8 + fc) * P:
                                                    (mi * 8 + fc + 1) * P],
                                        hr_ap(j, fc),
                                        start=(fc == 0), stop=(fc == 7))
                                d_o = sp.inc(inst)
                                wait(v_, d_o)
                                inst = v_.tensor_tensor(Xc(m, tb * QB, QB),
                                                        Xc(m, tb * QB, QB),
                                                        pb[bi][:, 0:QB], ALU.add)
                                pbdep[bi] = sv.inc(inst)
                        wr_free[wsl2] = d_o
                    h_free = sp.now()
            abf.reset(m0)
            af32.reset(mf)

        scb_ln = abf.alloc(NCH * QB)
        scrl_ln = af32.alloc(QB)
        base_abf, base_f32 = abf.mark(), af32.mark()

        if "l0a" in blocks:
            l0_attn()
            if dbg != 2:
                postln(scb=scb_ln, scrl=scrl_ln)
        if "l0f" in blocks:
            l0_ffn()
            postln(scb=scb_ln, scrl=scrl_ln)
        if "l1a" in blocks:
            l1_attn()
            postln(scb=scb_ln, scrl=scrl_ln)
        if "l1f" in blocks:
            l1_ffn()
            postln(scb=scb_ln, scrl=scrl_ln)

        d_fin = sv.now()
        wait(sy, d_fin, sa.now())
        d_y = dgrp().inc(sy.dma_start(out=y_out.rearrange("(c p) t -> p c t", p=P),
                                      in_=X[:].rearrange("p (c t) -> p c t", t=T)), 16)
        wait(sy, d_y)

    with nc.Block() as block:
        @block.sync
        def _(sy):
            emit(sy)

    stk.close()
    return nc



def _prep_l0a(iv, T, parity, rows):
    out = {}
    out["l0a_wq"] = _tw(iv["L0_attn_wq"]); out["l0a_bq"] = _bpm(iv["L0_attn_bq"])
    out["l0a_wqc"] = _tw(iv["L0_attn_wqc"]); out["l0a_bqc"] = _bpm(iv["L0_attn_bqc"])
    perm = np.concatenate([np.arange(64, 128), np.arange(0, 64)])
    out["l0a_wqr"] = _tw(iv["L0_attn_wqr"]); out["l0a_bqr"] = _bpm(iv["L0_attn_bqr"])
    out["l0a_wqrs"] = _tw(iv["L0_attn_wqr"][:, perm])
    out["l0a_bqrs"] = _bpm(iv["L0_attn_bqr"][perm])
    out["l0a_wkv"] = _tw(iv["L0_attn_wkv"]); out["l0a_bkv"] = _bpm(iv["L0_attn_bkv"])
    out["l0a_wkr"] = _tw(iv["L0_attn_wkr"]); out["l0a_bkr"] = _bpm(iv["L0_attn_bkr"])
    out["l0a_wkrs"] = _tw(iv["L0_attn_wkr"][:, perm])
    out["l0a_bkrs"] = _bpm(iv["L0_attn_bkr"][perm])
    out["l0a_wkc"] = _tw(iv["L0_attn_wkc"]); out["l0a_bkc"] = _bpm(iv["L0_attn_bkc"])
    out["l0a_wv"] = _trhs(iv["L0_attn_wv"])
    out["l0a_bvb"] = np.ascontiguousarray(
        np.broadcast_to(iv["L0_attn_bv"][None, :], (P, D)))
    out["l0a_wo"] = _who(iv["L0_attn_wo"]); out["l0a_bo"] = _bpm(iv["L0_attn_bo"])
    pos = rows.astype(np.float64)
    inv = 1.0 / (10000.0 ** (np.arange(0, 128, 2, dtype=np.float64) / 128.0))
    fr = (pos[None, :] * inv[:, None])
    cos = np.cos(fr).astype(np.float32)
    sin = np.sin(fr).astype(np.float32)
    out["l0a_cos"] = np.ascontiguousarray(np.concatenate([cos, cos], 0))
    out["l0a_sin"] = np.ascontiguousarray(np.concatenate([-sin, sin], 0))
    QBl = min(512, T)
    KTBl = QBl // P
    m = np.zeros((2 * KTBl, P, QBl), np.float32)
    for j in range(2 * KTBl):
        gk = P * j + np.arange(P)[:, None]
        gq = QBl * parity + np.arange(QBl)[None, :]
        m[j] = (gk <= gq).astype(np.float32)
    out["l0a_mask"] = m
    return out


def _prep_l0f(iv):
    out = {}
    out["l0f_g1"] = _bpm(iv["L0_ffn_ln1_w"])
    out["l0f_g2"] = _bpm(iv["L0_ffn_ln2_w"])
    out["l0f_wq"] = _twm(np.ascontiguousarray(iv["L0_ffn_wq"].T))
    out["l0f_wk"] = _twm(np.ascontiguousarray(iv["L0_ffn_wk"].T))
    out["l0f_wv"] = _trhs(np.ascontiguousarray(iv["L0_ffn_wv"].T))
    out["l0f_wo"] = _who(np.ascontiguousarray(iv["L0_ffn_wo"].T))
    out["l0f_gate"] = _trhs(np.ascontiguousarray(iv["L0_ffn_gate_w"].T), n=8)
    w1 = np.stack([_tw(np.ascontiguousarray(iv["L0_ffn_w1"][e].T))
                   .reshape(8, 16, P, P) for e in range(8)])
    w3 = np.stack([_tw(np.ascontiguousarray(iv["L0_ffn_w3"][e].T))
                   .reshape(8, 16, P, P) for e in range(8)])
    out["l0f_w13"] = np.ascontiguousarray(
        np.stack([np.stack([np.stack([w1[e][:, 2 * fp], w3[e][:, 2 * fp],
                                      w1[e][:, 2 * fp + 1], w3[e][:, 2 * fp + 1]],
                                     axis=0)
                            for fp in range(8)]) for e in range(8)]))
    w2 = np.stack([_tw(np.ascontiguousarray(iv["L0_ffn_w2"][e].T))
                   .reshape(16, 8, P, P) for e in range(8)])
    out["l0f_w2p"] = np.ascontiguousarray(
        np.stack([np.stack([np.stack([w2[e][:, 2 * mp], w2[e][:, 2 * mp + 1]],
                                     axis=0)
                            for mp in range(4)]) for e in range(8)]))
    return out


def _prep_const():
    out = {}
    oh8 = np.zeros((8, 8, P), np.float32)
    for e in range(8):
        oh8[e, e, :] = 1.0
    out["ident8"] = oh8
    oh16 = np.zeros((16, 16, P), np.float32)
    for e in range(16):
        oh16[e, e, :] = 1.0
    out["ident16"] = oh16
    out["identt"] = np.eye(P, dtype=np.float32)
    return out


def _prep_l1a(iv, T, rows):
    out = {}
    perm = np.arange(D).reshape(-1, 2)[:, ::-1].reshape(-1)
    wq = np.ascontiguousarray(iv["L1_attn_wq_w"].T)
    wk = np.ascontiguousarray(iv["L1_attn_wk_w"].T)
    out["l1a_wq"] = _twm(wq); out["l1a_bq"] = _bpm(iv["L1_attn_wq_b"])
    out["l1a_wqs"] = _twm(np.ascontiguousarray(wq[:, perm]))
    out["l1a_bqs"] = _bpm(iv["L1_attn_wq_b"][perm])
    out["l1a_wk"] = _twm(wk); out["l1a_bk"] = _bpm(iv["L1_attn_wk_b"])
    out["l1a_wks"] = _twm(np.ascontiguousarray(wk[:, perm]))
    out["l1a_bks"] = _bpm(iv["L1_attn_wk_b"][perm])
    out["l1a_wv"] = _trhs(np.ascontiguousarray(iv["L1_attn_wv_w"].T))
    out["l1a_bvb"] = np.ascontiguousarray(
        np.broadcast_to(iv["L1_attn_wv_b"][None, :], (P, D)))
    inv = 1.0 / (10000.0 ** (np.arange(0, D, 2, dtype=np.float64) / D))
    ang = rows.astype(np.float64)[None, :] * inv[:, None]
    cosI = np.empty((D, len(rows)), np.float32)
    sinI = np.empty((D, len(rows)), np.float32)
    cosI[0::2] = np.cos(ang); cosI[1::2] = np.cos(ang)
    sinI[0::2] = -np.sin(ang); sinI[1::2] = np.sin(ang)
    out["l1a_cos"] = np.ascontiguousarray(cosI.reshape(8, P, len(rows)))
    out["l1a_sin"] = np.ascontiguousarray(sinI.reshape(8, P, len(rows)))
    return out


def _prep_l1f(iv):
    out = {}
    out["l1f_router"] = _trhs(np.ascontiguousarray(iv["L1_ffn_router_w"].T), n=16)
    wgu = np.stack([_tw(np.ascontiguousarray(iv["L1_ffn_gate_up_proj"][e]))
                    .reshape(8, 16, P, P) for e in range(16)])
    out["l1f_gu"] = np.ascontiguousarray(
        np.stack([np.stack([np.stack([wgu[e][:, fc], wgu[e][:, fc + 8]], axis=0)
                            for fc in range(8)]) for e in range(16)]))
    wd = np.stack([_tw(np.ascontiguousarray(iv["L1_ffn_down_proj"][e]))
                   .reshape(8, 8, P, P) for e in range(16)])
    out["l1f_wdp"] = np.ascontiguousarray(
        np.stack([np.stack([np.stack([wd[e][:, 2 * mp], wd[e][:, 2 * mp + 1]],
                                     axis=0)
                            for mp in range(4)]) for e in range(16)]))
    sg = _tw(np.ascontiguousarray(iv["L1_ffn_shared_gate_w"].T)).reshape(8, 32, P, P)
    su = _tw(np.ascontiguousarray(iv["L1_ffn_shared_up_w"].T)).reshape(8, 32, P, P)
    out["l1f_sgu"] = np.ascontiguousarray(
        np.stack([np.stack([sg[:, fc], su[:, fc]], axis=0) for fc in range(32)]))
    out["l1f_sd"] = _tw(np.ascontiguousarray(iv["L1_ffn_shared_down_w"].T)).reshape(32, 8, P, P)
    return out


def make_inputs(iv, T=2048, blocks=("l0a", "l0f", "l1a", "l1f")):
    x = np.asarray(iv["L0_attn_x"])
    nn = np.array([[np.random.randint(1, 2 ** 30)]], dtype=np.int32)
    shared = {"nonce": nn}
    if "l0f" in blocks:
        shared.update(_prep_l0f(iv))
    if "l1f" in blocks:
        shared.update(_prep_l1f(iv))
    shared.update(_prep_const())
    par = [dict(shared) for _ in range(2)]
    for parity in range(2):
        rows = own_rows(parity, T)
        if "l0a" in blocks:
            par[parity].update(_prep_l0a(iv, T, parity, rows))
        if "l1a" in blocks:
            par[parity].update(_prep_l1a(iv, T, rows))
    maps = []
    for core in range(8):
        b = core // 2
        parity = core % 2
        rows = own_rows(parity, T)
        im = dict(par[parity])
        im["x_fm"] = np.ascontiguousarray(np.asarray(x[b])[rows].T)
        maps.append(im)
    return maps


def assemble(results, T=2048, B=4, S=4096):
    out = np.zeros((B, S, D), np.float32)
    for core in range(8):
        b = core // 2
        rows = own_rows(core % 2, T)
        out[b][rows] = results[core]["y_fm"].T
    return out


def kernel(**inputs):
    T = 2048
    blocks = ("l0a", "l0f", "l1a", "l1f")
    maps = make_inputs(inputs, T=T, blocks=blocks)
    nc = build_program(T=T, blocks=blocks)
    res = run_bass_kernel_spmd(nc, maps, core_ids=list(range(8)))
    return assemble(res.results, T=T)
